# version 16
# baseline (speedup 1.0000x reference)
from contextlib import ExitStack

import numpy as np
import ml_dtypes

import concourse.bass as bass
import concourse.mybir as mybir
from concourse.bass_utils import run_bass_kernel_spmd

F32 = mybir.dt.float32
BF16 = mybir.dt.bfloat16
I16 = mybir.dt.int16
U8 = mybir.dt.uint8
AF = mybir.ActivationFunctionType
ALU = mybir.AluOpType
AX = mybir.AxisListType

D = 2048
S = 2048
NH = 16
HD = 128
NM = 16384
NCORE = 8

NEG = -3.0e38
RS = 0.08838834764831845
GC1 = 0.7978845608028654
GC3 = GC1 * 0.044715
EPS = 1e-5

SLOPES = [2.0 ** (-0.5 * (h + 1)) for h in range(NH)]

_TRG_SEQ = [(hh, qq, jj) for hh in range(NH) for qq in range(16) for jj in range((qq + 4) // 4)]
_TRG_IDX = {t: i for i, t in enumerate(_TRG_SEQ)}


class Ctr:
    def __init__(self, sem):
        self.sem = sem
        self.v = 0

    def inc(self, inst, n=1):
        inst.then_inc(self.sem, n)
        self.v += n
        return self.v

    def dma(self, inst):
        return self.inc(inst, 16)


def build(debug_out=(), stop_after=None):
    nc = bass.Bass("TRN2", target_bir_lowering=False)

    ap_x = nc.dram_tensor("x", [S, D], F32, kind="ExternalInput").ap()
    ap_wq = nc.dram_tensor("wq", [D, D], F32, kind="ExternalInput").ap()
    ap_wk = nc.dram_tensor("wk", [D, D], F32, kind="ExternalInput").ap()
    ap_wv = nc.dram_tensor("wv", [D, D], F32, kind="ExternalInput").ap()
    ap_wo = nc.dram_tensor("wo", [D, D], F32, kind="ExternalInput").ap()
    ap_bq = nc.dram_tensor("bq", [128, 16], F32, kind="ExternalInput").ap()
    ap_bk = nc.dram_tensor("bk", [128, 16], F32, kind="ExternalInput").ap()
    ap_bvb = nc.dram_tensor("bvb", [128, D], F32, kind="ExternalInput").ap()
    ap_bob = nc.dram_tensor("bob", [128, D], F32, kind="ExternalInput").ap()
    ap_pkg = nc.dram_tensor("pkg", [4096, 4096], U8, kind="ExternalInput").ap()
    ap_pku = nc.dram_tensor("pku", [4096, 4096], U8, kind="ExternalInput").ap()
    ap_pkd = nc.dram_tensor("pkd", [4096, 4096], U8, kind="ExternalInput").ap()
    ap_scg = nc.dram_tensor("scg", [4096, 4096], F32, kind="ExternalInput").ap()
    ap_scu = nc.dram_tensor("scu", [4096, 4096], F32, kind="ExternalInput").ap()
    ap_scd = nc.dram_tensor("scd", [4096, 4096], F32, kind="ExternalInput").ap()
    ap_ident = nc.dram_tensor("ident", [128, 128], BF16, kind="ExternalInput").ap()
    ap_tmt = nc.dram_tensor("tmt", [128, S], F32, kind="ExternalInput").ap()
    ap_negafq = nc.dram_tensor("negafq", [128, 256], F32, kind="ExternalInput").ap()
    ap_alibi = nc.dram_tensor("alibi", [128, S], F32, kind="ExternalInput").ap()

    ap_out = nc.dram_tensor("out", [S, D], F32, kind="ExternalOutput").ap()
    kind_dbg = lambda name: "ExternalOutput" if name in debug_out else "Internal"
    ap_x1 = nc.dram_tensor("x1", [S, D], F32, kind=kind_dbg("x1")).ap()
    ap_x2 = nc.dram_tensor("x2", [S, D], F32, kind=kind_dbg("x2")).ap()
    ap_intr = nc.dram_tensor("intr", [NM, S], BF16, kind=kind_dbg("intr")).ap()
    ap_qt = nc.dram_tensor("qt", [D, S], BF16, kind=kind_dbg("qt")).ap()
    ap_kt = nc.dram_tensor("kt", [D, S], BF16, kind=kind_dbg("kt")).ap()
    ap_vt = nc.dram_tensor("vt", [S, D], BF16, kind=kind_dbg("vt")).ap()
    ap_xtd = (nc.dram_tensor("xtd", [D, S], BF16, kind="ExternalOutput").ap()
              if "xtd" in debug_out else None)
    ap_w0d = (nc.dram_tensor("w0d", [128, 4 * 4096], BF16, kind="ExternalOutput").ap()
              if "w0d" in debug_out else None)
    ap_b16d = (nc.dram_tensor("b16d", [128, 4096], I16, kind="ExternalOutput").ap()
               if "w0d" in debug_out else None)

    with ExitStack() as top:
        def mkctr(name):
            return Ctr(top.enter_context(nc.semaphore(name)))

        cIN = mkctr("cIN")
        cOUT = mkctr("cOUT")
        cPE = mkctr("cPE")
        cVE = mkctr("cVE")
        cAC = mkctr("cAC")
        cX = [mkctr("cXA"), mkctr("cXB")]
        cPKG, cPKU = mkctr("cPKG"), mkctr("cPKU")
        cIT = [mkctr("cITA"), mkctr("cITB")]
        cPKD, cSCD, cACC = mkctr("cPKD"), mkctr("cSCD"), mkctr("cACC")
        cSC2 = [mkctr("cSC2A"), mkctr("cSC2B")]
        cSXF = [mkctr("cSXFA"), mkctr("cSXFB")]
        cWF = [mkctr("cWFA"), mkctr("cWFB")]
        cKQ = [mkctr("cKQA"), mkctr("cKQB")]
        cONE = mkctr("cONE")
        cIS = [mkctr("cISA"), mkctr("cISB")]
        cST = [mkctr("cSTA"), mkctr("cSTB")]
        cFL = mkctr("cFL")
        cFO = [mkctr("cFOA"), mkctr("cFOB")]
        OUT_CTRS = [cOUT, cIS[0], cIS[1], cST[0], cST[1], cFL, cFO[0], cFO[1]]

        def wait_outs(e):
            for c_ in OUT_CTRS:
                if c_.v:
                    e.wait_ge(c_.sem, c_.v)

        _uq = [0]

        def sbuf(scope, name, shape, dt):
            _uq[0] += 1
            return scope.enter_context(nc.sbuf_tensor(f"{name}_{_uq[0]}", shape, dt))

        def psum(scope, name, shape, dt):
            _uq[0] += 1
            return scope.enter_context(nc.psum_tensor(f"{name}_{_uq[0]}", shape, dt))

        def block(name):
            _uq[0] += 1
            return nc.Block(f"{name}_{_uq[0]}")

        sb_id = top.enter_context(nc.sbuf_tensor("sb_id", [128, 128], BF16))
        sb_bq = top.enter_context(nc.sbuf_tensor("sb_bq", [128, 16], F32))
        sb_bk = top.enter_context(nc.sbuf_tensor("sb_bk", [128, 16], F32))
        sb_bqs = top.enter_context(nc.sbuf_tensor("sb_bqs", [128, 16], F32))
        sb_cst = top.enter_context(nc.sbuf_tensor("sb_cst", [128, 2], F32))

        with nc.Block("preload") as blk:
            @blk.sync
            def _(e):
                for dst, src in ((sb_id, ap_ident),
                                 (sb_bq, ap_bq), (sb_bk, ap_bk)):
                    cIN.dma(e.dma_start(out=dst[:, :], in_=src))
                e.wait_ge(cIN.sem, cIN.v)

            @blk.vector
            def _(v):
                v.wait_ge(cIN.sem, cIN.v)
                v.memset(sb_cst[:, 0:1], EPS)
                v.memset(sb_cst[:, 1:2], RS)
                cVE.inc(v.tensor_scalar(sb_bqs[:, :], sb_bq[:, :], RS, None, ALU.mult))

        def ln_transpose(ap_src, sb_xt):
            with ExitStack() as ph, block("lnt") as blk:
                b_x = [sbuf(ph, f"lx{i}", [128, D], F32) for i in range(2)]
                b_sq = sbuf(ph, "lsq", [128, D], F32)
                b_xn = [sbuf(ph, f"lxn{i}", [128, D], BF16) for i in range(2)]
                st_sum = sbuf(ph, "lsum", [128, 1], F32)
                st_ssq = sbuf(ph, "lssq", [128, 1], F32)
                st_m = sbuf(ph, "lm", [128, 1], F32)
                st_t1 = sbuf(ph, "lt1", [128, 1], F32)
                st_t2 = sbuf(ph, "lt2", [128, 1], F32)
                st_rstd = sbuf(ph, "lrstd", [128, 1], F32)
                st_nmr = sbuf(ph, "lnmr", [128, 1], F32)
                p_tr = [psum(ph, f"ltr{i}", [128, 128], BF16) for i in range(4)]

                ac0, ve0, pe0 = cAC.v, cVE.v, cPE.v
                xb = [cX[0].v, cX[1].v]
                AC_T = 2
                VE_T = 1 + 16
                PE_T = 16
                _xw = lambda i: xb[i % 2] + 16 * (i // 2 + 1)

                @blk.sync
                def _(e):
                    wait_outs(e)
                    for i in range(16):
                        if i >= 2:
                            e.wait_ge(cAC.sem, ac0 + (i - 2) * AC_T + 2)
                            e.wait_ge(cVE.sem, ve0 + (i - 2) * VE_T + 1)
                        cX[i % 2].dma(e.dma_start(out=b_x[i % 2][:, :], in_=ap_src[i * 128:(i + 1) * 128, :]))

                @blk.scalar
                def _(sc):
                    for i in range(16):
                        sc.wait_ge(cX[i % 2].sem, _xw(i))
                        if i >= 1:
                            sc.wait_ge(cVE.sem, ve0 + (i - 1) * VE_T + 1)
                        cAC.inc(sc.activation(b_sq[:, :], b_x[i % 2][:, :], AF.Square))
                        sc.wait_ge(cVE.sem, ve0 + i * VE_T + 1)
                        sc.activation(st_m[:, :], st_sum[:, :], AF.Identity, scale=1.0 / D)
                        sc.activation(st_t1[:, :], st_m[:, :], AF.Square)
                        sc.activation(st_t1[:, :], st_t1[:, :], AF.Identity, scale=-1.0, bias=sb_cst[:, 0:1])
                        sc.activation(st_t2[:, :], st_m[:, :], AF.Identity, scale=-1.0)
                        sc.activation(st_t1[:, :], st_ssq[:, :], AF.Ln, scale=1.0 / D, bias=st_t1[:, :])
                        sc.activation(st_rstd[:, :], st_t1[:, :], AF.Exp, scale=-0.5)
                        sc.activation(st_t1[:, :], st_t1[:, :], AF.Identity)
                        sc.activation(st_nmr[:, :], st_t2[:, :], AF.Identity, scale=st_rstd[:, :])
                        sc.activation(st_t1[:, :], st_t1[:, :], AF.Identity)
                        if i >= 2:
                            sc.wait_ge(cPE.sem, pe0 + (i - 1) * PE_T)
                        cAC.inc(sc.activation(b_xn[i % 2][:, :], b_x[i % 2][:, :], AF.Identity,
                                              scale=st_rstd[:, :], bias=st_nmr[:, :]))

                @blk.vector
                def _(v):
                    for i in range(16):
                        v.wait_ge(cX[i % 2].sem, _xw(i))
                        if i >= 1:
                            v.wait_ge(cAC.sem, ac0 + (i - 1) * AC_T + 2)
                        v.reduce_sum(st_sum[:, :], b_x[i % 2][:, :], AX.X)
                        v.wait_ge(cAC.sem, ac0 + i * AC_T + 1)
                        cVE.inc(v.reduce_sum(st_ssq[:, :], b_sq[:, :], AX.X))
                        for c in range(16):
                            v.wait_ge(cPE.sem, pe0 + i * PE_T + c + 1)
                            cVE.inc(v.tensor_copy(sb_xt[c][:, i * 128:(i + 1) * 128], p_tr[c % 4][:, :]))

                @blk.tensor
                def _(t):
                    for i in range(16):
                        t.wait_ge(cAC.sem, ac0 + i * AC_T + 2)
                        for c in range(16):
                            k = i * 16 + c
                            if k >= 4:
                                i4, c4 = (k - 4) // 16, (k - 4) % 16
                                t.wait_ge(cVE.sem, ve0 + i4 * VE_T + 1 + c4 + 1)
                            cPE.inc(t.transpose(p_tr[c % 4][:, :], b_xn[i % 2][:, 128 * c:128 * (c + 1)], sb_id[:, :]))

                assert cX[0].v == xb[0] + 128 and cX[1].v == xb[1] + 128
                assert cAC.v == ac0 + 16 * AC_T
                assert cVE.v == ve0 + 16 * VE_T
                assert cPE.v == pe0 + 16 * PE_T

        def emit_decode(v, b_src_i16, b_s1, b_s2, b_s3, b_s4, plane, wdst_ap):
            if plane == 0:
                v.tensor_scalar(b_s1, b_src_i16, 7, None, ALU.bitwise_and)
                v.tensor_scalar(b_s2, b_src_i16, 8, 2, ALU.bitwise_and, ALU.logical_shift_right)
            else:
                v.tensor_scalar(b_s1, b_src_i16, 4, 7, ALU.logical_shift_right, ALU.bitwise_and)
                v.tensor_scalar(b_s2, b_src_i16, 6, 2, ALU.logical_shift_right, ALU.bitwise_and)
            v.tensor_scalar(b_s3, b_s1, 2, -4, ALU.mult, ALU.add)
            v.tensor_tensor(b_s4, b_s1, b_s3, ALU.max)
            v.tensor_scalar(b_s3, b_s4, 2, -8, ALU.mult, ALU.add)
            v.tensor_tensor(b_s1, b_s4, b_s3, ALU.max)
            v.tensor_scalar(b_s2, b_s2, -1, 1, ALU.mult, ALU.add)
            v.tensor_tensor(b_s4, b_s1, b_s2, ALU.mult)

        def ffn(ap_src, ap_dst, dump_xt=False, skip_c=False):
            with ExitStack() as sub:
                sb_xt = [sbuf(sub, f"xt{c}", [128, S], BF16) for c in range(16)]
                ln_transpose(ap_src, sb_xt)
                if dump_xt:
                    with block("xtd") as blk:
                        @blk.gpsimd
                        def _(g_):
                            for c in range(16):
                                cOUT.dma(g_.dma_start(out=ap_xtd[128 * c:128 * (c + 1), :], in_=sb_xt[c][:, :]))
                            g_.wait_ge(cOUT.sem, cOUT.v)

                with ExitStack() as ph, block("ffnb") as blk:
                    b_pk = [sbuf(ph, f"fpk{m}", [128, 4096], U8) for m in range(2)]
                    b_sxs = [sbuf(ph, f"fsxs{i}", [128, 1024], F32) for i in range(2)]
                    b_sxu = sbuf(ph, "fsxu", [128, 4096], BF16)
                    b_sxg = sbuf(ph, "fsxg", [128, 4096], BF16)
                    b_b0 = sbuf(ph, "fb0", [128, 1024], I16)
                    b_s1 = sbuf(ph, "fs1", [128, 1024], I16)
                    b_s2 = sbuf(ph, "fs2", [128, 1024], I16)
                    b_s3 = sbuf(ph, "fs3", [128, 1024], I16)
                    b_s4 = sbuf(ph, "fs4", [128, 1024], I16)
                    b_wg = [[sbuf(ph, f"fwg{pp}{p}", [128, 4096], BF16) for p in range(2)] for pp in range(2)]
                    b_wu = [[sbuf(ph, f"fwu{pp}{p}", [128, 4096], BF16) for p in range(2)] for pp in range(2)]
                    b_gs = [sbuf(ph, f"fgs{i}", [128, 512], F32) for i in range(2)]
                    b_g2 = [sbuf(ph, f"fg2{i}", [128, 512], F32) for i in range(2)]
                    b_u1 = sbuf(ph, "fu1", [128, 512], F32)
                    b_v1 = sbuf(ph, "fv1", [128, 512], F32)
                    b_is = [sbuf(ph, f"fis{i}", [128, S], BF16) for i in range(2)]
                    p_g = [psum(ph, f"fpg{i}", [128, 512], F32) for i in range(3)]
                    p_u = [psum(ph, f"fpu{i}", [128, 512], F32) for i in range(3)]

                    ac0, ve0, pe0 = cAC.v, cVE.v, cPE.v
                    pkg0, pku0 = cPKG.v, cPKU.v
                    sxf0 = [cSXF[0].v, cSXF[1].v]
                    isb = [cIS[0].v, cIS[1].v]
                    NS = 32

                    ac_ms, ve_ms, pe_ms = {}, {}, {}
                    a = ac0
                    for qt_ in range(4):
                        a += 1; ac_ms[("cxu", 0, qt_)] = a
                    for qt_ in range(4):
                        a += 1; ac_ms[("cxg", 0, qt_)] = a
                    for s in range(NS):
                        for i in range(16):
                            if i % 2 == 0 and s + 1 < NS:
                                a += 1
                                ac_ms[("cxu", s + 1, i // 2) if i < 8 else ("cxg", s + 1, (i - 8) // 2)] = a
                            a += 1; ac_ms[("sq", s, i)] = a
                            a += 1; ac_ms[("th", s, i)] = a
                    ac_end = a
                    v_ = ve0
                    for mat in range(2):
                        for j in range(8):
                            if j == 6:
                                v_ += 1; ve_ms[("cu8" if mat == 0 else "cg8", 0)] = v_
                            v_ += 1; ve_ms[("uch" if mat == 0 else "gch", 0, j)] = v_
                    for s in range(NS):
                        for i in range(16):
                            v_ += 1; ve_ms[("u", s, i)] = v_
                            v_ += 1; ve_ms[("it", s, i)] = v_
                            if s + 1 < NS:
                                j = i if i < 8 else i - 8
                                if j == 6:
                                    v_ += 1; ve_ms[("cu8" if i < 8 else "cg8", s + 1)] = v_
                                v_ += 1; ve_ms[("uch" if i < 8 else "gch", s + 1, j)] = v_
                    ve_end = v_
                    p_ = pe0
                    for s in range(NS):
                        for i in range(16):
                            p_ += 1; pe_ms[("g", s, i)] = p_
                            p_ += 1; pe_ms[("up", s, i)] = p_
                    pe_end = p_

                    def slot_key(q):
                        s_, r_ = q // 8, q % 8
                        return ("cxu", s_, r_) if r_ < 4 else ("cxg", s_, r_ - 4)

                    @blk.sync
                    def _(e):
                        wait_outs(e)
                        for m in range(NS):
                            if m >= 1:
                                e.wait_ge(cVE.sem, ve_ms[("cu8", m - 1)])
                            cPKU.dma(e.dma_start(out=b_pk[1][:, :], in_=ap_pku[128 * m:128 * (m + 1), :]))
                            for r in range(4):
                                q = 8 * m + r
                                if q >= 2:
                                    e.wait_ge(cAC.sem, ac_ms[slot_key(q - 2)])
                                cSXF[q % 2].dma(e.dma_start(
                                    out=b_sxs[q % 2][:, :],
                                    in_=ap_scu[128 * m:128 * (m + 1), 1024 * r:1024 * (r + 1)]))
                            if m >= 1:
                                e.wait_ge(cVE.sem, ve_ms[("cg8", m - 1)])
                            cPKG.dma(e.dma_start(out=b_pk[0][:, :], in_=ap_pkg[128 * m:128 * (m + 1), :]))
                            for r in range(4):
                                q = 8 * m + 4 + r
                                if q >= 2:
                                    e.wait_ge(cAC.sem, ac_ms[slot_key(q - 2)])
                                cSXF[q % 2].dma(e.dma_start(
                                    out=b_sxs[q % 2][:, :],
                                    in_=ap_scg[128 * m:128 * (m + 1), 1024 * r:1024 * (r + 1)]))

                    @blk.scalar
                    def _(sc):
                        def castx(s, mat, qt_):
                            q = 8 * s + 4 * mat + qt_
                            sc.wait_ge(cSXF[q % 2].sem, sxf0[q % 2] + 16 * (q // 2 + 1))
                            if s >= 1:
                                sc.wait_ge(cVE.sem, ve_ms[("uch" if mat == 0 else "gch", s - 1, 2 * qt_ + 1)])
                            dst = b_sxu if mat == 0 else b_sxg
                            cAC.inc(sc.activation(dst[:, 1024 * qt_:1024 * (qt_ + 1)], b_sxs[q % 2][:, :],
                                                  AF.Identity, scale=0.5))

                        for qt_ in range(4):
                            castx(0, 0, qt_)
                        for qt_ in range(4):
                            castx(0, 1, qt_)
                        for s in range(NS):
                            for i in range(16):
                                if i % 2 == 0 and s + 1 < NS:
                                    castx(s + 1, 0 if i < 8 else 1, (i % 8) // 2)
                                sc.wait_ge(cPE.sem, pe_ms[("g", s, i)])
                                if i >= 2:
                                    sc.wait_ge(cVE.sem, ve_ms[("it", s, i - 2)])
                                elif s >= 1:
                                    sc.wait_ge(cVE.sem, ve_ms[("it", s - 1, 14 + i)])
                                sc.activation(b_gs[i % 2][:, :], p_g[(16 * s + i) % 3][:, :], AF.Copy)
                                cAC.inc(sc.activation(b_g2[i % 2][:, :], p_g[(16 * s + i) % 3][:, :], AF.Square))
                                sc.wait_ge(cVE.sem, ve_ms[("u", s, i)])
                                cAC.inc(sc.activation(b_g2[i % 2][:, :], b_u1[:, :], AF.Tanh))

                    @blk.tensor
                    def _(t):
                        for s in range(NS):
                            for i in range(16):
                                nt, tsl = i // 4, i % 4
                                if i == 0:
                                    t.wait_ge(cVE.sem, ve_ms[("gch", s, 7)])
                                if i >= 3:
                                    t.wait_ge(cAC.sem, ac_ms[("sq", s, i - 3)])
                                    t.wait_ge(cVE.sem, ve_ms[("it", s, i - 3)])
                                elif s >= 1:
                                    t.wait_ge(cAC.sem, ac_ms[("sq", s - 1, 13 + i)])
                                    t.wait_ge(cVE.sem, ve_ms[("it", s - 1, 13 + i)])
                                for c in range(16):
                                    w = b_wg[s % 2][0 if c < 8 else 1]
                                    cc = c % 8
                                    inst = t.matmul(p_g[(16 * s + i) % 3][:, :],
                                                    w[:, 512 * cc + 128 * nt:512 * cc + 128 * (nt + 1)],
                                                    sb_xt[c][:, 512 * tsl:512 * (tsl + 1)],
                                                    start=(c == 0), stop=(c == 15))
                                    if c == 15:
                                        cPE.inc(inst)
                                if i == 0:
                                    t.wait_ge(cVE.sem, ve_ms[("uch", s, 7)])
                                for c in range(16):
                                    w = b_wu[s % 2][0 if c < 8 else 1]
                                    cc = c % 8
                                    inst = t.matmul(p_u[(16 * s + i) % 3][:, :],
                                                    w[:, 512 * cc + 128 * nt:512 * cc + 128 * (nt + 1)],
                                                    sb_xt[c][:, 512 * tsl:512 * (tsl + 1)],
                                                    start=(c == 0), stop=(c == 15))
                                    if c == 15:
                                        cPE.inc(inst)

                    @blk.vector
                    def _(v):
                        def chunk(v, s, mat, j):
                            qt_, plane = j // 2, j % 2
                            sl = slice(1024 * qt_, 1024 * (qt_ + 1))
                            if j == 0:
                                if mat == 0:
                                    v.wait_ge(cPKU.sem, pku0 + 16 * (s + 1))
                                    if s >= 2:
                                        v.wait_ge(cPE.sem, pe_ms[("up", s - 2, 15)])
                                else:
                                    v.wait_ge(cPKG.sem, pkg0 + 16 * (s + 1))
                                    if s >= 2:
                                        v.wait_ge(cPE.sem, pe_ms[("g", s - 2, 15)])
                            if plane == 0:
                                inst = v.tensor_copy(b_b0[:, :], b_pk[1 - mat][:, sl])
                                if j == 6:
                                    cVE.inc(inst)
                            emit_decode(v, b_b0[:, :], b_s1[:, :], b_s2[:, :],
                                        b_s3[:, :], b_s4[:, :], plane, None)
                            v.wait_ge(cAC.sem, ac_ms[("cxu" if mat == 0 else "cxg", s, j // 2)])
                            wdst = (b_wu if mat == 0 else b_wg)[s % 2][plane]
                            bsx = b_sxu if mat == 0 else b_sxg
                            cVE.inc(v.tensor_tensor(wdst[:, sl], b_s4[:, :], bsx[:, sl], ALU.mult))

                        for mat in range(2):
                            for j in range(8):
                                chunk(v, 0, mat, j)
                        for s in range(NS):
                            for i in range(16):
                                nt, tsl = i // 4, i % 4
                                gnt = 4 * s + nt
                                v.wait_ge(cAC.sem, ac_ms[("sq", s, i)])
                                v.tensor_scalar(b_v1[:, :], b_g2[i % 2][:, :], GC3, GC1, ALU.mult, ALU.add)
                                cVE.inc(v.tensor_tensor(b_u1[:, :], b_v1[:, :], b_gs[i % 2][:, :], ALU.mult))
                                v.wait_ge(cAC.sem, ac_ms[("th", s, i)])
                                v.tensor_scalar(b_v1[:, :], b_g2[i % 2][:, :], 0.5, 0.5, ALU.mult, ALU.add)
                                v.tensor_tensor(b_v1[:, :], b_v1[:, :], b_gs[i % 2][:, :], ALU.mult)
                                v.wait_ge(cPE.sem, pe_ms[("up", s, i)])
                                if tsl == 0 and gnt >= 2:
                                    v.wait_ge(cIS[gnt % 2].sem, isb[gnt % 2] + 16 * (gnt // 2))
                                cVE.inc(v.tensor_tensor(b_is[gnt % 2][:, 512 * tsl:512 * (tsl + 1)],
                                                        b_v1[:, :], p_u[(16 * s + i) % 3][:, :], ALU.mult))
                                if s + 1 < NS:
                                    chunk(v, s + 1, 0 if i < 8 else 1, i if i < 8 else i - 8)

                    @blk.gpsimd
                    def _(g):
                        for s in range(NS):
                            for nt in range(4):
                                gnt = 4 * s + nt
                                g.wait_ge(cVE.sem, ve_ms[("it", s, nt * 4 + 3)])
                                cIS[gnt % 2].dma(g.dma_start(out=ap_intr[128 * gnt:128 * (gnt + 1), :], in_=b_is[gnt % 2][:, :]))

                    assert cPKG.v == pkg0 + 512 and cPKU.v == pku0 + 512
                    assert cSXF[0].v == sxf0[0] + 16 * 128 and cSXF[1].v == sxf0[1] + 16 * 128
                    assert cAC.v == ac_end, (cAC.v, ac_end)
                    assert cVE.v == ve_end, (cVE.v, ve_end)
                    assert cPE.v == pe_end, (cPE.v, pe_end)
                    assert cIS[0].v == isb[0] + 64 * 16 and cIS[1].v == isb[1] + 64 * 16

            if skip_c:
                return
            with ExitStack() as ph, block("ffnc") as blk:
                b_acc = [sbuf(ph, f"dac{i}", [128, 1024], F32) for i in range(16)]
                b_it = [sbuf(ph, f"dit{i}", [128, S], BF16) for i in range(16)]
                b_pk = sbuf(ph, "dpk", [128, 4096], U8)
                sxs = [sbuf(ph, f"dsxs{i}", [128, 1024], F32) for i in range(2)]
                b_sx = sbuf(ph, "dsx", [128, 4096], BF16)
                b_b16 = sbuf(ph, "db16", [128, 4096], I16)
                b_s1 = sbuf(ph, "ds1", [128, 2048], I16)
                b_s2 = sbuf(ph, "ds2", [128, 2048], I16)
                b_s3 = sbuf(ph, "ds3", [128, 2048], I16)
                b_s4 = sbuf(ph, "ds4", [128, 2048], I16)
                b_wd = [[sbuf(ph, f"dwd{pp}{p}", [128, 4096], BF16) for p in range(2)] for pp in range(2)]
                p_d = [psum(ph, f"dpd{i}", [128, 1024], F32) for i in range(3)]

                ac0, ve0, pe0 = cAC.v, cVE.v, cPE.v
                it0 = [cIT[0].v, cIT[1].v]
                pkd0, acc0, fl0 = cPKD.v, cACC.v, cFL.v
                sc20 = [cSC2[0].v, cSC2[1].v]
                NG = 32

                ac_ms, ve_ms, pe_ms = {}, {}, {}
                a = ac0
                for c in range(4):
                    a += 1; ac_ms[("cxd", 0, c)] = a
                for m in range(NG):
                    if m + 1 < NG:
                        for c in range(4):
                            a += 1; ac_ms[("cxd", m + 1, c)] = a
                ac_end = a
                v_ = ve0
                v_ += 1; ve_ms[("c16", 0)] = v_
                for c in range(4):
                    v_ += 1; ve_ms[("dw", 0, c)] = v_
                for m in range(NG):
                    for tt in range(16):
                        v_ += 1; ve_ms[("ac", m, tt)] = v_
                        if tt % 4 == 1 and m + 1 < NG:
                            c = tt // 4
                            if c == 0:
                                v_ += 1; ve_ms[("c16", m + 1)] = v_
                            v_ += 1; ve_ms[("dw", m + 1, c)] = v_
                ve_end = v_
                p_ = pe0
                for m in range(NG):
                    for tt in range(16):
                        p_ += 1; pe_ms[("mm", m, tt)] = p_
                pe_end = p_

                @blk.sync
                def _(e):
                    wait_outs(e)
                    for m in range(NG):
                        h2, g = m // 16, m % 16
                        if m >= 2:
                            e.wait_ge(cPE.sem, pe_ms[("mm", m - 2, 15)])
                        for k in range(8):
                            j = 4 * g + k if k < 4 else 64 + 4 * g + (k - 4)
                            slot = (m % 2) * 8 + k
                            cIT[m % 2].dma(e.dma_start(out=b_it[slot][:, :], in_=ap_intr[128 * j:128 * (j + 1), :]))
                        if m >= 1:
                            e.wait_ge(cVE.sem, ve_ms[("c16", m - 1)])
                        cPKD.dma(e.dma_start(out=b_pk[:, :],
                                             in_=ap_pkd[256 * g + 128 * h2:256 * g + 128 * (h2 + 1), :]))
                        for c in range(4):
                            q = 4 * m + c
                            if q >= 2:
                                q2 = q - 2
                                e.wait_ge(cAC.sem, ac_ms[("cxd", q2 // 4, q2 % 4)])
                            cSC2[q % 2].dma(e.dma_start(
                                out=sxs[q % 2][:, :],
                                in_=ap_scd[128 * m:128 * (m + 1), 1024 * c:1024 * (c + 1)]))
                        if g == 0:
                            if h2 == 1:
                                e.wait_ge(cFL.sem, fl0 + 16 * 16)
                            for i in range(16):
                                cACC.dma(e.dma_start(out=b_acc[i][:, :],
                                                     in_=ap_src[128 * i:128 * (i + 1), 1024 * h2:1024 * (h2 + 1)]))

                @blk.scalar
                def _(sc):
                    def chain(m):
                        for c in range(4):
                            q = 4 * m + c
                            sc.wait_ge(cSC2[q % 2].sem, sc20[q % 2] + 16 * (q // 2 + 1))
                            if m >= 1:
                                sc.wait_ge(cVE.sem, ve_ms[("dw", m - 1, 2 * (c // 2) + 1)])
                            cAC.inc(sc.activation(b_sx[:, 1024 * c:1024 * (c + 1)], sxs[q % 2][:, :],
                                                  AF.Identity, scale=0.25))

                    chain(0)
                    for m in range(NG):
                        if m + 1 < NG:
                            chain(m + 1)

                @blk.tensor
                def _(t):
                    for m in range(NG):
                        for tt in range(16):
                            if tt == 0:
                                t.wait_ge(cIT[m % 2].sem, it0[m % 2] + 128 * (m // 2 + 1))
                                t.wait_ge(cVE.sem, ve_ms[("dw", m, 3)])
                            if tt >= 3:
                                t.wait_ge(cVE.sem, ve_ms[("ac", m, tt - 3)])
                            elif m >= 1:
                                t.wait_ge(cVE.sem, ve_ms[("ac", m - 1, 13 + tt)])
                            for k in range(8):
                                c4, plane = k % 4, k // 4
                                slot = (m % 2) * 8 + (c4 if plane == 0 else 4 + c4)
                                wsrc = b_wd[m % 2][plane]
                                for j in range(2):
                                    inst = t.matmul(p_d[(16 * m + tt) % 3][:, 512 * j:512 * (j + 1)],
                                                    b_it[slot][:, 128 * tt:128 * (tt + 1)],
                                                    wsrc[:, 1024 * c4 + 512 * j:1024 * c4 + 512 * (j + 1)],
                                                    start=(k == 0), stop=(k == 7))
                                    if k == 7 and j == 1:
                                        cPE.inc(inst)

                @blk.vector
                def _(v):
                    def chunk(v, m, c):
                        half, plane = c // 2, c % 2
                        sl = slice(2048 * half, 2048 * (half + 1))
                        if c == 0:
                            v.wait_ge(cPKD.sem, pkd0 + 16 * (m + 1))
                            if m >= 2:
                                v.wait_ge(cPE.sem, pe_ms[("mm", m - 2, 15)])
                            cVE.inc(v.tensor_copy(b_b16[:, :], b_pk[:, :]))
                        emit_decode(v, b_b16[:, sl], b_s1[:, :], b_s2[:, :], b_s3[:, :], b_s4[:, :], plane, None)
                        v.wait_ge(cAC.sem, ac_ms[("cxd", m, 2 * half + 1)])
                        cVE.inc(v.tensor_tensor(b_wd[m % 2][plane][:, sl], b_s4[:, :], b_sx[:, sl], ALU.mult))

                    for c in range(4):
                        chunk(v, 0, c)
                    for m in range(NG):
                        h2, g = m // 16, m % 16
                        for tt in range(16):
                            v.wait_ge(cPE.sem, pe_ms[("mm", m, tt)])
                            if tt == 0 and g == 0:
                                v.wait_ge(cACC.sem, acc0 + 256 * (h2 + 1))
                            cVE.inc(v.tensor_tensor(b_acc[tt][:, :], b_acc[tt][:, :],
                                                    p_d[(16 * m + tt) % 3][:, :], ALU.add))
                            if tt % 4 == 1 and m + 1 < NG:
                                chunk(v, m + 1, tt // 4)

                @blk.gpsimd
                def _(g_):
                    for h2 in range(2):
                        for i in range(16):
                            g_.wait_ge(cVE.sem, ve_ms[("ac", h2 * 16 + 15, i)])
                            cFL.dma(g_.dma_start(out=ap_dst[128 * i:128 * (i + 1), 1024 * h2:1024 * (h2 + 1)],
                                                 in_=b_acc[i][:, :]))

                assert cIT[0].v == it0[0] + 128 * 16 and cIT[1].v == it0[1] + 128 * 16
                assert cPKD.v == pkd0 + 512
                assert cSC2[0].v == sc20[0] + 16 * 64 and cSC2[1].v == sc20[1] + 16 * 64
                assert cACC.v == acc0 + 512
                assert cAC.v == ac_end, (cAC.v, ac_end)
                assert cVE.v == ve_end, (cVE.v, ve_end)
                assert cPE.v == pe_end, (cPE.v, pe_end)
                assert cFL.v == fl0 + 512

        def attn(ap_src, ap_dst):
            with ExitStack() as sub:
                with ExitStack() as ph:
                    sb_xt = [sbuf(ph, f"axt{c}", [128, S], BF16) for c in range(16)]
                    ln_transpose(ap_src, sb_xt)

                    with ExitStack() as ph2, block("proj") as blk:
                        b_wf = [sbuf(ph2, f"awf{i}", [128, D], F32) for i in range(2)]
                        b_wb = [sbuf(ph2, f"awb{c}", [128, D], BF16) for c in range(16)]
                        b_st = [sbuf(ph2, f"ast{i}", [128, 512], BF16) for i in range(2)]
                        sb_bvb = sbuf(ph2, "sb_bvb", [128, D], F32)
                        p_p = [psum(ph2, f"app{i}", [128, 512], F32) for i in range(2)]

                        ac0, ve0, pe0 = cAC.v, cVE.v, cPE.v
                        wf0 = [cWF[0].v, cWF[1].v]
                        one0 = cONE.v
                        w_src = [ap_wq, ap_wk, ap_wv]
                        o_dst = [ap_qt, ap_kt, None]
                        st_hist = [cST[0].v, cST[1].v]
                        st_base = [cST[0].v, cST[1].v]
                        st_wait = {}
                        for m_ in range(3):
                            for g_ in range(64):
                                st_wait[(m_, g_)] = st_hist[g_ % 2]
                                st_hist[g_ % 2] += 16 * (1 if m_ < 2 else 4)

                        def _castv(m, c):
                            return ve0 + 16 * m + c + 1

                        def _vevac(g):
                            return ve0 + 48 + g + 1

                        @blk.sync
                        def _(e):
                            wait_outs(e)
                            cONE.dma(e.dma_start(out=sb_bvb[:, :], in_=ap_bvb))
                            for m in range(3):
                                for c in range(16):
                                    k = m * 16 + c
                                    if k >= 2:
                                        e.wait_ge(cVE.sem, _castv((k - 2) // 16, (k - 2) % 16))
                                    cWF[k % 2].dma(e.dma_start(out=b_wf[c % 2][:, :],
                                                               in_=w_src[m][128 * c:128 * (c + 1), :]))

                        @blk.vector
                        def _(v):
                            v.wait_ge(cONE.sem, one0 + 16)
                            for m in range(3):
                                for c in range(16):
                                    k = m * 16 + c
                                    v.wait_ge(cWF[k % 2].sem, wf0[k % 2] + 16 * (k // 2 + 1))
                                    if m > 0 and c == 0:
                                        v.wait_ge(cPE.sem, pe0 + m * 64)
                                    cVE.inc(v.tensor_copy(b_wb[c][:, :], b_wf[c % 2][:, :]))
                            for g in range(64):
                                tt_i, j = g // 4, g % 4
                                v.wait_ge(cPE.sem, pe0 + 2 * 64 + g + 1)
                                if st_wait[(2, g)] > st_base[g % 2]:
                                    v.wait_ge(cST[g % 2].sem, st_wait[(2, g)])
                                cVE.inc(v.scalar_tensor_tensor(
                                    out=b_st[g % 2][:, :],
                                    in0=sb_bvb[:, 512 * j:512 * (j + 1)], scalar=1.0,
                                    in1=p_p[g % 2][:, :], op0=ALU.mult, op1=ALU.add))

                        @blk.tensor
                        def _(t):
                            for m in range(3):
                                t.wait_ge(cVE.sem, _castv(m, 15))
                                for g in range(64):
                                    ot, j = g // 4, g % 4
                                    G = m * 64 + g
                                    if G >= 2:
                                        pm, pg2 = divmod(G - 2, 64)
                                        if pm < 2:
                                            t.wait_ge(cAC.sem, ac0 + pm * 64 + pg2 + 1)
                                        else:
                                            t.wait_ge(cVE.sem, _vevac(pg2))
                                    for c in range(16):
                                        if m < 2:
                                            inst = t.matmul(p_p[g % 2][:, :],
                                                            b_wb[c][:, 128 * ot:128 * (ot + 1)],
                                                            sb_xt[c][:, 512 * j:512 * (j + 1)],
                                                            start=(c == 0), stop=(c == 15))
                                        else:
                                            inst = t.matmul(p_p[g % 2][:, :],
                                                            sb_xt[c][:, 128 * ot:128 * (ot + 1)],
                                                            b_wb[c][:, 512 * j:512 * (j + 1)],
                                                            start=(c == 0), stop=(c == 15))
                                        if c == 15:
                                            cPE.inc(inst)

                        @blk.scalar
                        def _(sc):
                            for m in range(2):
                                bias = sb_bqs if m == 0 else sb_bk
                                for g in range(64):
                                    ot = g // 4
                                    sc.wait_ge(cPE.sem, pe0 + m * 64 + g + 1)
                                    if st_wait[(m, g)] > st_base[g % 2]:
                                        sc.wait_ge(cST[g % 2].sem, st_wait[(m, g)])
                                    if m == 0:
                                        cAC.inc(sc.activation(b_st[g % 2][:, :], p_p[g % 2][:, :], AF.Identity,
                                                              bias=bias[:, ot:ot + 1], scale=sb_cst[:, 1:2]))
                                    else:
                                        cAC.inc(sc.activation(b_st[g % 2][:, :], p_p[g % 2][:, :], AF.Identity,
                                                              bias=bias[:, ot:ot + 1]))

                        @blk.gpsimd
                        def _(g_):
                            for m in range(2):
                                dst = o_dst[m]
                                for g in range(64):
                                    ot, j = g // 4, g % 4
                                    g_.wait_ge(cAC.sem, ac0 + m * 64 + g + 1)
                                    cST[g % 2].dma(g_.dma_start(out=dst[128 * ot:128 * (ot + 1), 512 * j:512 * (j + 1)],
                                                                in_=b_st[g % 2][:, :]))
                            for g in range(64):
                                ot, j = g // 4, g % 4
                                g_.wait_ge(cVE.sem, _vevac(g))
                                for hh in range(4):
                                    cST[g % 2].dma(g_.dma_start(
                                        out=ap_vt[(4 * j + hh) * 128:(4 * j + hh + 1) * 128, 128 * ot:128 * (ot + 1)],
                                        in_=b_st[g % 2][:, 128 * hh:128 * (hh + 1)]))

                        assert cWF[0].v == wf0[0] + 24 * 16 and cWF[1].v == wf0[1] + 24 * 16
                        assert cVE.v == ve0 + 48 + 64
                        assert cPE.v == pe0 + 3 * 64
                        assert cAC.v == ac0 + 2 * 64
                        assert cST[0].v == st_hist[0] and cST[1].v == st_hist[1]

                sb_ctx = [sbuf(sub, f"actx{h}", [128, S], BF16) for h in range(NH)]
                with ExitStack() as ph, block("heads") as blk:
                    b_kh = [sbuf(ph, f"akh{i}", [128, S], BF16) for i in range(2)]
                    b_qh = [sbuf(ph, f"aqh{i}", [128, S], BF16) for i in range(2)]
                    b_vh = [sbuf(ph, f"avh{i}", [128, S], BF16) for i in range(2)]
                    b_p3 = [sbuf(ph, f"ap3{i}", [128, S], BF16) for i in range(2)]
                    sb_al = sbuf(ph, "sb_al", [128, S], F32)
                    b_af = sbuf(ph, "aaf", [128, S], F32)
                    b_amdt = sbuf(ph, "aamdt", [128, S], F32)
                    b_ptt = sbuf(ph, "aptt", [128, 16 * 512], BF16)
                    b_sst = sbuf(ph, "asst", [128, 6], F32)
                    sb_tmt = sbuf(ph, "sb_tmt", [128, S], F32)
                    sb_nq = sbuf(ph, "sb_nq", [128, 256], F32)
                    p_s = psum(ph, "aps", [128, 2048], F32)
                    p_tw = [psum(ph, f"aptw{i}", [128, 512], BF16) for i in range(2)]
                    p_av = psum(ph, "apav", [128, 512], F32)

                    ac0, ve0, pe0 = cAC.v, cVE.v, cPE.v
                    kq0 = [cKQ[0].v, cKQ[1].v]
                    one0 = cONE.v
                    pt3 = b_ptt[:, :].rearrange("p (kc u) -> p kc u", u=512)

                    ITERS = [(h, qi) for h in range(NH) for qi in range(16)]
                    prev_of = {ITERS[i]: ITERS[i - 1] for i in range(1, len(ITERS))}
                    prev_of[ITERS[0]] = None
                    UNITS = [(it, prev_of[it]) for it in ITERS] + [(None, ITERS[-1])]

                    pe_sched, ve_sched, ac_sched = {}, {}, {}
                    pe_c, ve_c, ac_c = pe0, ve0, ac0
                    for it, pq in UNITS:
                        if it is not None:
                            nb_ = (it[1] + 4) // 4
                            for b in range(nb_):
                                pe_c += 1
                                pe_sched[("scb",) + it + (b,)] = pe_c
                            pe_sched[("score",) + it] = pe_c
                        if pq is not None:
                            hp, qp = pq
                            for j in range((qp + 4) // 4):
                                pe_c += 1
                                pe_sched[("trg",) + pq + (j,)] = pe_c
                            if qp % 4 == 3:
                                pe_c += 1
                                pe_sched[("av", hp, qp // 4)] = pe_c
                        if it is not None:
                            nb_ = (it[1] + 4) // 4
                            for b in range(nb_ - 1):
                                ve_c += 1
                                ve_sched[("addb",) + it + (b,)] = ve_c
                            ve_c += 1
                            ve_sched[("diag",) + it] = ve_c
                        if pq is not None:
                            hp, qp = pq
                            if qp % 4 == 0:
                                ve_c += 1
                                ve_sched[("ms", hp, qp // 4)] = ve_c
                        if it is not None:
                            ve_c += 1
                            ve_sched[("rcp",) + it] = ve_c
                            if 128 * (it[1] + 1) > 512:
                                ac_c += 1
                                ac_sched[("exp1",) + it] = ac_c
                            ac_c += 1
                            ac_sched[("exp",) + it] = ac_c
                        if pq is not None:
                            for j in range((pq[1] + 4) // 4):
                                ac_c += 1
                                ac_sched[("tge",) + pq + (j,)] = ac_c
                        if it is not None:
                            ac_c += 1
                            ac_sched[("soft",) + it] = ac_c
                        if pq is not None and pq[1] % 4 == 3:
                            ac_c += 1
                            ac_sched[("avev", pq[0], pq[1] // 4)] = ac_c

                    @blk.sync
                    def _(e):
                        wait_outs(e)
                        cONE.dma(e.dma_start(out=sb_tmt[:, :], in_=ap_tmt))
                        cONE.dma(e.dma_start(out=sb_nq[:, :], in_=ap_negafq))
                        cONE.dma(e.dma_start(out=sb_al[:, :], in_=ap_alibi))
                        for h in range(NH):
                            if h >= 2:
                                e.wait_ge(cPE.sem, pe_sched[("av", h - 2, 3)])
                            cKQ[h % 2].dma(e.dma_start(out=b_kh[h % 2][:, :], in_=ap_kt[128 * h:128 * (h + 1), :]))
                            cKQ[h % 2].dma(e.dma_start(out=b_qh[h % 2][:, :], in_=ap_qt[128 * h:128 * (h + 1), :]))
                            cKQ[h % 2].dma(e.dma_start(out=b_vh[h % 2][:, :], in_=ap_vt[128 * h:128 * (h + 1), :]))

                    @blk.tensor
                    def _(t):
                        for it, pq in UNITS:
                            if it is not None:
                                h, qi = it
                                if qi == 0:
                                    t.wait_ge(cKQ[h % 2].sem, kq0[h % 2] + 48 * (h // 2 + 1))
                                klen = 128 * (qi + 1)
                                nb = (klen + 511) // 512
                                nbp = (128 * (pq[1] + 1) + 511) // 512 if pq is not None else 0
                                for b in range(nb):
                                    if pq is not None and b == 0:
                                        t.wait_ge(cAC.sem, ac_sched[("exp1",) + pq] if nbp > 1
                                                  else ac_sched[("exp",) + pq])
                                    if pq is not None and b == nbp - 1:
                                        t.wait_ge(cAC.sem, ac_sched[("exp",) + pq])
                                    w = min(512, klen - 512 * b)
                                    cPE.inc(t.matmul(p_s[:, 512 * b:512 * b + w],
                                                     b_qh[h % 2][:, 128 * qi:128 * (qi + 1)],
                                                     b_kh[h % 2][:, 512 * b:512 * b + w],
                                                     start=True, stop=True))
                            if pq is None:
                                continue
                            hp, qp = pq
                            t.wait_ge(cAC.sem, ac_sched[("soft",) + pq])
                            for kc in range(qp + 1):
                                j = kc // 4
                                gi = _TRG_IDX[pq + (j,)]
                                if kc % 4 == 0 and gi >= 2:
                                    t.wait_ge(cAC.sem, ac_sched[("tge",) + _TRG_SEQ[gi - 2]])
                                inst = t.transpose(p_tw[gi % 2][:, 128 * (kc % 4):128 * (kc % 4 + 1)],
                                                   b_p3[qp % 2][:, 128 * kc:128 * (kc + 1)], sb_id[:, :])
                                if kc % 4 == 3 or kc == qp:
                                    cPE.inc(inst)
                            if qp % 4 == 3:
                                g = qp // 4
                                t.wait_ge(cAC.sem, ac_sched[("tge",) + pq + (qp // 4,)])
                                if (hp, g) != (0, 0):
                                    pg = (hp, g - 1) if g > 0 else (hp - 1, 3)
                                    t.wait_ge(cAC.sem, ac_sched[("avev",) + pg])
                                for kc in range(4 * (g + 1)):
                                    inst = t.matmul(p_av[:, :], b_vh[hp % 2][:, 128 * kc:128 * (kc + 1)],
                                                    b_ptt[:, 512 * kc:512 * (kc + 1)],
                                                    start=(kc == 0), stop=(kc == 4 * g + 3))
                                    if kc == 4 * g + 3:
                                        cPE.inc(inst)

                    @blk.vector
                    def _(v):
                        for it, pq in UNITS:
                            if it is not None:
                                h, qi = it
                                if qi == 0:
                                    if h == 0:
                                        v.wait_ge(cONE.sem, one0 + 48)
                                    slope = SLOPES[h]
                                    v.tensor_scalar(b_af[:, :], sb_al[:, :], slope, None, ALU.mult)
                                    v.tensor_tensor(b_amdt[:, :], b_af[:, :], sb_tmt[:, :], ALU.add)
                                dg = slice(qi * 128, qi * 128 + 128)
                                b_d = (qi * 128) // 512
                                o = qi * 128 - 512 * b_d
                                for b in range(b_d):
                                    v.wait_ge(cPE.sem, pe_sched[("scb",) + it + (b,)])
                                    cVE.inc(v.tensor_tensor(p_s[:, 512 * b:512 * (b + 1)],
                                                            p_s[:, 512 * b:512 * (b + 1)],
                                                            b_af[:, 512 * b:512 * (b + 1)], ALU.add))
                                v.wait_ge(cPE.sem, pe_sched[("score",) + it])
                                if o > 0:
                                    v.tensor_tensor(p_s[:, 512 * b_d:512 * b_d + o],
                                                    p_s[:, 512 * b_d:512 * b_d + o],
                                                    b_af[:, 512 * b_d:512 * b_d + o], ALU.add)
                                cVE.inc(v.tensor_tensor(p_s[:, dg], p_s[:, dg], b_amdt[:, dg], ALU.add))
                            if pq is not None:
                                hp, qp = pq
                                if qp % 4 == 0:
                                    g = qp // 4
                                    if (hp, g) != (0, 0):
                                        pg = (hp, g - 1) if g > 0 else (hp - 1, 3)
                                        v.wait_ge(cPE.sem, pe_sched[("av",) + pg])
                                    if g < 3:
                                        cVE.inc(v.memset(b_ptt[:, 512 * (4 * g + 1):512 * (4 * g + 4)], 0))
                                    else:
                                        cVE.inc(v.memset(b_ptt[:, 512 * 13:512 * 16], 0))
                            if it is not None:
                                h, qi = it
                                nb = (128 * (qi + 1) + 511) // 512
                                v.wait_ge(cAC.sem, ac_sched[("exp",) + it])
                                if nb > 1:
                                    v.tensor_tensor(b_sst[:, 3:4], b_sst[:, 4:5], b_sst[:, 5:6], ALU.add)
                                    v.drain()
                                cVE.inc(v.reciprocal(b_sst[:, 2:3], b_sst[:, 3:4]))

                    @blk.scalar
                    def _(sc):
                        for it, pq in UNITS:
                            if it is not None:
                                h, qi = it
                                klen = 128 * (qi + 1)
                                nb = (klen + 511) // 512
                                bias = sb_nq[:, 16 * h + qi:16 * h + qi + 1]
                                if nb > 1:
                                    sc.wait_ge(cVE.sem, ve_sched[("addb",) + it + (nb - 2,)])
                                    cAC.inc(sc.activation(b_p3[qi % 2][:, 0:512 * (nb - 1)], p_s[:, 0:512 * (nb - 1)],
                                                          AF.Exp, bias=bias, accum_out=b_sst[:, 4:5]))
                                sc.wait_ge(cVE.sem, ve_sched[("diag",) + it])
                                cAC.inc(sc.activation(b_p3[qi % 2][:, 512 * (nb - 1):klen],
                                                      p_s[:, 512 * (nb - 1):klen], AF.Exp, bias=bias,
                                                      accum_out=b_sst[:, 5:6] if nb > 1 else b_sst[:, 3:4]))
                            if pq is not None:
                                hp, qp = pq
                                g4 = qp // 4
                                for j in range((qp + 4) // 4):
                                    gi = _TRG_IDX[pq + (j,)]
                                    sc.wait_ge(cPE.sem, pe_sched[("trg",) + pq + (j,)])
                                    if j == g4 and qp % 4 != 0:
                                        sc.wait_ge(cVE.sem, ve_sched[("ms", hp, g4)])
                                    nk = min(qp, 4 * j + 3) - 4 * j + 1
                                    cAC.inc(sc.activation(
                                        pt3[:, 4 * j:4 * j + nk, 128 * (qp % 4):128 * (qp % 4 + 1)],
                                        p_tw[gi % 2][:, 0:128 * nk].rearrange("p (t u) -> p t u", u=128),
                                        AF.Copy))
                            if it is not None:
                                h, qi = it
                                klen = 128 * (qi + 1)
                                sc.wait_ge(cVE.sem, ve_sched[("rcp",) + it])
                                cAC.inc(sc.activation(b_p3[qi % 2][:, 0:klen], b_p3[qi % 2][:, 0:klen],
                                                      AF.Identity, scale=b_sst[:, 2:3]))
                            if pq is not None and pq[1] % 4 == 3:
                                sc.wait_ge(cPE.sem, pe_sched[("av", pq[0], pq[1] // 4)])
                                cAC.inc(sc.activation(sb_ctx[pq[0]][:, 512 * (pq[1] // 4):512 * (pq[1] // 4 + 1)],
                                                      p_av[:, :], AF.Copy))

                    assert cPE.v == pe_c, (cPE.v, pe_c)
                    assert cVE.v == ve_c, (cVE.v, ve_c)
                    assert cAC.v == ac_c, (cAC.v, ac_c)
                    assert cKQ[0].v == kq0[0] + 48 * 8 and cKQ[1].v == kq0[1] + 48 * 8
                with ExitStack() as ph, block("oproj") as blk:
                    b_wf = [sbuf(ph, f"owf{i}", [128, D], F32) for i in range(2)]
                    b_wb = [sbuf(ph, f"owb{c}", [128, D], BF16) for c in range(16)]
                    b_x1 = [sbuf(ph, f"ox1{i}", [128, D], F32) for i in range(2)]
                    b_o = [sbuf(ph, f"oo{i}", [128, D], F32) for i in range(2)]
                    sb_bob = sbuf(ph, "sb_bob", [128, D], F32)
                    p_p = [psum(ph, f"opp{i}", [128, 512], F32) for i in range(2)]

                    ac0, ve0, pe0 = cAC.v, cVE.v, cPE.v
                    wf0 = [cWF[0].v, cWF[1].v]
                    x0 = [cX[0].v, cX[1].v]
                    fo0 = [cFO[0].v, cFO[1].v]
                    one0 = cONE.v
                    _vc = lambda c: ve0 + c + 1
                    _vx = lambda i: ve0 + 16 + 5 * i + 1
                    _va = lambda i, j: ve0 + 16 + 5 * i + 1 + j + 1

                    @blk.sync
                    def _(e):
                        wait_outs(e)
                        cONE.dma(e.dma_start(out=sb_bob[:, :], in_=ap_bob))
                        for c in range(16):
                            if c >= 2:
                                e.wait_ge(cVE.sem, _vc(c - 2))
                            cWF[c % 2].dma(e.dma_start(out=b_wf[c % 2][:, :], in_=ap_wo[128 * c:128 * (c + 1), :]))
                        for i in range(16):
                            if i >= 2:
                                e.wait_ge(cVE.sem, _vx(i - 2))
                            cX[i % 2].dma(e.dma_start(out=b_x1[i % 2][:, :], in_=ap_src[128 * i:(i + 1) * 128, :]))

                    @blk.vector
                    def _(v):
                        v.wait_ge(cONE.sem, one0 + 16)
                        for c in range(16):
                            v.wait_ge(cWF[c % 2].sem, wf0[c % 2] + 16 * (c // 2 + 1))
                            cVE.inc(v.tensor_copy(b_wb[c][:, :], b_wf[c % 2][:, :]))
                        for i in range(16):
                            v.wait_ge(cX[i % 2].sem, x0[i % 2] + 16 * (i // 2 + 1))
                            if i >= 2:
                                v.wait_ge(cFO[i % 2].sem, fo0[i % 2] + 16 * (i // 2))
                            cVE.inc(v.tensor_tensor(b_o[i % 2][:, :], b_x1[i % 2][:, :], sb_bob[:, :], ALU.add))
                            for j in range(4):
                                v.wait_ge(cPE.sem, pe0 + 4 * i + j + 1)
                                cVE.inc(v.tensor_tensor(b_o[i % 2][:, 512 * j:512 * (j + 1)],
                                                        b_o[i % 2][:, 512 * j:512 * (j + 1)],
                                                        p_p[j % 2][:, :], ALU.add))

                    @blk.tensor
                    def _(t):
                        t.wait_ge(cVE.sem, _vc(15))
                        for i in range(16):
                            for j in range(4):
                                g = 4 * i + j
                                if g >= 2:
                                    t.wait_ge(cVE.sem, _va((g - 2) // 4, (g - 2) % 4))
                                for h in range(16):
                                    inst = t.matmul(p_p[j % 2][:, :], sb_ctx[h][:, 128 * i:128 * (i + 1)],
                                                    b_wb[h][:, 512 * j:512 * (j + 1)],
                                                    start=(h == 0), stop=(h == 15))
                                    if h == 15:
                                        cPE.inc(inst)

                    @blk.gpsimd
                    def _(g_):
                        for i in range(16):
                            g_.wait_ge(cVE.sem, _va(i, 3))
                            cFO[i % 2].dma(g_.dma_start(out=ap_dst[128 * i:128 * (i + 1), :], in_=b_o[i % 2][:, :]))

                    assert cWF[0].v == wf0[0] + 8 * 16 and cWF[1].v == wf0[1] + 8 * 16
                    assert cX[0].v == x0[0] + 8 * 16 and cX[1].v == x0[1] + 8 * 16
                    assert cVE.v == ve0 + 16 + 16 * 5
                    assert cPE.v == pe0 + 64
                    assert cFO[0].v == fo0[0] + 8 * 16 and cFO[1].v == fo0[1] + 8 * 16

        ffn(ap_x, ap_x1, dump_xt="xtd" in debug_out, skip_c=(stop_after == "ffnb"))
        if stop_after not in ("ffn1", "ffnb"):
            attn(ap_x1, ap_x2)
            ffn(ap_x2, ap_out)

        with nc.Block("fin") as blk:
            @blk.gpsimd
            def _(g_):
                for c_ in OUT_CTRS:
                    if c_.v:
                        g_.wait_ge(c_.sem, c_.v)

    return nc



_DPERM = np.concatenate([np.arange(0, D, 2), np.arange(1, D, 2)])
_NPERM = np.concatenate([np.arange(0, NM, 2), np.arange(1, NM, 2)])
_INV_DPERM = np.argsort(_DPERM)


def _bytes_of_packed(arr_i32, n_rows, n_half):
    a = np.ascontiguousarray(np.asarray(arr_i32, dtype=np.int32))
    return a.view(np.uint8).reshape(n_rows, n_half * 4)[:, ::4]


def host_prep(inputs):
    x = np.asarray(inputs["L0_attn_x"], np.float32)
    wq = np.ascontiguousarray(np.asarray(inputs["L0_attn_Wq"], np.float32)[_DPERM, :])
    wk = np.ascontiguousarray(np.asarray(inputs["L0_attn_Wk"], np.float32)[_DPERM, :])
    wv = np.ascontiguousarray(np.asarray(inputs["L0_attn_Wv"], np.float32)[_DPERM, :])
    wo = np.ascontiguousarray(np.asarray(inputs["L0_attn_Wo"], np.float32)[:, _DPERM])
    bq = np.ascontiguousarray(np.asarray(inputs["L0_attn_bq"], np.float32).reshape(16, 128).T)
    bk = np.ascontiguousarray(np.asarray(inputs["L0_attn_bk"], np.float32).reshape(16, 128).T)
    bvb = np.ascontiguousarray(np.broadcast_to(np.asarray(inputs["L0_attn_bv"], np.float32), (128, D)))
    bob = np.ascontiguousarray(np.broadcast_to(np.asarray(inputs["L0_attn_bo"], np.float32)[_DPERM], (128, D)))

    pkg = _bytes_of_packed(inputs["L0_ffn_gate_W_packed"], NM, 1024)
    pku = _bytes_of_packed(inputs["L0_ffn_up_W_packed"], NM, 1024)
    pkd = _bytes_of_packed(inputs["L0_ffn_down_W_packed"], D, 8192)
    pkgT = pkg.T[:, _NPERM]
    pkuT = pku.T[:, _NPERM]
    pkdT = pkd.T[:, _DPERM]
    pkgR = np.ascontiguousarray(pkgT.reshape(8, 128, 32, 512).transpose(2, 1, 0, 3).reshape(4096, 4096))
    pkuR = np.ascontiguousarray(pkuT.reshape(8, 128, 32, 512).transpose(2, 1, 0, 3).reshape(4096, 4096))
    pkdR = np.ascontiguousarray(pkdT.reshape(16, 4, 128, 2, 1024).transpose(0, 3, 2, 1, 4).reshape(4096, 4096))

    sg = np.asarray(inputs["L0_ffn_gate_scales"], np.float32)[_NPERM, :].T
    su = np.asarray(inputs["L0_ffn_up_scales"], np.float32)[_NPERM, :].T
    sd = np.asarray(inputs["L0_ffn_down_scales"], np.float32)[_DPERM, :].T
    scg = np.ascontiguousarray(np.repeat(
        sg.reshape(8, 8, 32, 512).transpose(1, 2, 0, 3).reshape(8, 32, 4096).transpose(1, 0, 2),
        16, axis=1).reshape(4096, 4096))
    scu = np.ascontiguousarray(np.repeat(
        su.reshape(8, 8, 32, 512).transpose(1, 2, 0, 3).reshape(8, 32, 4096).transpose(1, 0, 2),
        16, axis=1).reshape(4096, 4096))
    scd = np.ascontiguousarray(np.repeat(
        sd.reshape(64, 8, D).transpose(1, 0, 2).reshape(8, 64, 2, 1024)
        .transpose(2, 1, 0, 3).reshape(2, 16, 4, 8, 1024).transpose(0, 1, 3, 2, 4),
        16, axis=2).reshape(4096, 4096))

    ident = np.eye(128, dtype=ml_dtypes.bfloat16)
    qv, kv = np.meshgrid(np.arange(128), np.arange(128), indexing="ij")
    trimask = np.where(kv <= qv, 0.0, NEG).astype(np.float32)
    dist = (np.arange(S) - (S - 1)).astype(np.float32)
    tmt = np.ascontiguousarray(np.tile(trimask, (1, S // 128)))
    alibi = np.ascontiguousarray(np.broadcast_to(dist, (128, S)))
    pgrid = np.arange(128).reshape(128, 1)
    qigrid = np.arange(16).reshape(1, 16)
    negafq = np.zeros((128, 256), np.float32)
    for h in range(NH):
        negafq[:, 16 * h:16 * (h + 1)] = -SLOPES[h] * (128 * qigrid + pgrid - (S - 1))

    shared = dict(wq=wq, wk=wk, wv=wv, wo=wo, bq=bq, bk=bk, bvb=bvb, bob=bob,
                  pkg=pkgR, pku=pkuR, pkd=pkdR, scg=scg, scu=scu, scd=scd,
                  ident=ident, tmt=tmt,
                  negafq=negafq, alibi=alibi)
    xs = [np.ascontiguousarray(x[b][:, _DPERM]) for b in range(NCORE)]
    return shared, xs


def kernel(**inputs):
    shared, xs = host_prep(inputs)
    nc = build()
    in_maps = [dict(shared, x=xs[b]) for b in range(NCORE)]
    res = run_bass_kernel_spmd(nc, in_maps, core_ids=list(range(NCORE)))
    out = np.stack([np.asarray(r["out"], np.float32) for r in res.results])
    return np.ascontiguousarray(out[:, :, _INV_DPERM])


# version 17
# speedup vs baseline: 1.0244x; 1.0244x over previous
from contextlib import ExitStack

import numpy as np
import ml_dtypes

import concourse.bass as bass
import concourse.mybir as mybir
from concourse.bass_utils import run_bass_kernel_spmd

F32 = mybir.dt.float32
BF16 = mybir.dt.bfloat16
I16 = mybir.dt.int16
U8 = mybir.dt.uint8
AF = mybir.ActivationFunctionType
ALU = mybir.AluOpType
AX = mybir.AxisListType

D = 2048
S = 2048
NH = 16
HD = 128
NM = 16384
NCORE = 8

NEG = -3.0e38
RS = 0.08838834764831845
GC1 = 0.7978845608028654
GC3 = GC1 * 0.044715
EPS = 1e-5

SLOPES = [2.0 ** (-0.5 * (h + 1)) for h in range(NH)]

_TRG_SEQ = [(hh, qq, jj) for hh in range(NH) for qq in range(16) for jj in range((qq + 4) // 4)]
_TRG_IDX = {t: i for i, t in enumerate(_TRG_SEQ)}


class Ctr:
    def __init__(self, sem):
        self.sem = sem
        self.v = 0

    def inc(self, inst, n=1):
        inst.then_inc(self.sem, n)
        self.v += n
        return self.v

    def dma(self, inst):
        return self.inc(inst, 16)


def build(debug_out=(), stop_after=None):
    nc = bass.Bass("TRN2", target_bir_lowering=False)

    ap_x = nc.dram_tensor("x", [S, D], F32, kind="ExternalInput").ap()
    ap_wq = nc.dram_tensor("wq", [D, D], F32, kind="ExternalInput").ap()
    ap_wk = nc.dram_tensor("wk", [D, D], F32, kind="ExternalInput").ap()
    ap_wv = nc.dram_tensor("wv", [D, D], F32, kind="ExternalInput").ap()
    ap_wo = nc.dram_tensor("wo", [D, D], F32, kind="ExternalInput").ap()
    ap_bq = nc.dram_tensor("bq", [128, 16], F32, kind="ExternalInput").ap()
    ap_bk = nc.dram_tensor("bk", [128, 16], F32, kind="ExternalInput").ap()
    ap_bvb = nc.dram_tensor("bvb", [128, D], F32, kind="ExternalInput").ap()
    ap_bob = nc.dram_tensor("bob", [128, D], F32, kind="ExternalInput").ap()
    ap_pkg = nc.dram_tensor("pkg", [4096, 4096], U8, kind="ExternalInput").ap()
    ap_pku = nc.dram_tensor("pku", [4096, 4096], U8, kind="ExternalInput").ap()
    ap_pkd = nc.dram_tensor("pkd", [4096, 4096], U8, kind="ExternalInput").ap()
    ap_scg = nc.dram_tensor("scg", [4096, 4096], F32, kind="ExternalInput").ap()
    ap_scu = nc.dram_tensor("scu", [4096, 4096], F32, kind="ExternalInput").ap()
    ap_scd = nc.dram_tensor("scd", [4096, 4096], F32, kind="ExternalInput").ap()
    ap_ident = nc.dram_tensor("ident", [128, 128], BF16, kind="ExternalInput").ap()
    ap_tmt = nc.dram_tensor("tmt", [128, S], F32, kind="ExternalInput").ap()
    ap_negafq = nc.dram_tensor("negafq", [128, 256], F32, kind="ExternalInput").ap()
    ap_alibi = nc.dram_tensor("alibi", [128, S], F32, kind="ExternalInput").ap()

    ap_out = nc.dram_tensor("out", [S, D], F32, kind="ExternalOutput").ap()
    kind_dbg = lambda name: "ExternalOutput" if name in debug_out else "Internal"
    ap_x1 = nc.dram_tensor("x1", [S, D], F32, kind=kind_dbg("x1")).ap()
    ap_x2 = nc.dram_tensor("x2", [S, D], F32, kind=kind_dbg("x2")).ap()
    ap_intr = nc.dram_tensor("intr", [NM, S], BF16, kind=kind_dbg("intr")).ap()
    ap_qt = nc.dram_tensor("qt", [D, S], BF16, kind=kind_dbg("qt")).ap()
    ap_kt = nc.dram_tensor("kt", [D, S], BF16, kind=kind_dbg("kt")).ap()
    ap_vt = nc.dram_tensor("vt", [S, D], BF16, kind=kind_dbg("vt")).ap()
    ap_xtd = (nc.dram_tensor("xtd", [D, S], BF16, kind="ExternalOutput").ap()
              if "xtd" in debug_out else None)
    ap_w0d = (nc.dram_tensor("w0d", [128, 4 * 4096], BF16, kind="ExternalOutput").ap()
              if "w0d" in debug_out else None)
    ap_b16d = (nc.dram_tensor("b16d", [128, 4096], I16, kind="ExternalOutput").ap()
               if "w0d" in debug_out else None)

    with ExitStack() as top:
        def mkctr(name):
            return Ctr(top.enter_context(nc.semaphore(name)))

        cIN = mkctr("cIN")
        cOUT = mkctr("cOUT")
        cPE = mkctr("cPE")
        cVE = mkctr("cVE")
        cAC = mkctr("cAC")
        cX = [mkctr("cXA"), mkctr("cXB")]
        cPKG, cPKU = mkctr("cPKG"), mkctr("cPKU")
        cIT = [mkctr("cITA"), mkctr("cITB")]
        cPKD, cSCD, cACC = mkctr("cPKD"), mkctr("cSCD"), mkctr("cACC")
        cSC2 = [mkctr("cSC2A"), mkctr("cSC2B")]
        cSXF = [mkctr("cSXFA"), mkctr("cSXFB")]
        cWF = [mkctr("cWFA"), mkctr("cWFB")]
        cKQ = [mkctr("cKQA"), mkctr("cKQB")]
        cONE = mkctr("cONE")
        cIS = [mkctr("cISA"), mkctr("cISB")]
        cST = [mkctr("cSTA"), mkctr("cSTB")]
        cFL = mkctr("cFL")
        cFO = [mkctr("cFOA"), mkctr("cFOB")]
        OUT_CTRS = [cOUT, cIS[0], cIS[1], cST[0], cST[1], cFL, cFO[0], cFO[1]]

        def wait_outs(e):
            for c_ in OUT_CTRS:
                if c_.v:
                    e.wait_ge(c_.sem, c_.v)

        _uq = [0]

        def sbuf(scope, name, shape, dt):
            _uq[0] += 1
            return scope.enter_context(nc.sbuf_tensor(f"{name}_{_uq[0]}", shape, dt))

        def psum(scope, name, shape, dt):
            _uq[0] += 1
            return scope.enter_context(nc.psum_tensor(f"{name}_{_uq[0]}", shape, dt))

        def block(name):
            _uq[0] += 1
            return nc.Block(f"{name}_{_uq[0]}")

        sb_id = top.enter_context(nc.sbuf_tensor("sb_id", [128, 128], BF16))
        sb_bq = top.enter_context(nc.sbuf_tensor("sb_bq", [128, 16], F32))
        sb_bk = top.enter_context(nc.sbuf_tensor("sb_bk", [128, 16], F32))
        sb_bqs = top.enter_context(nc.sbuf_tensor("sb_bqs", [128, 16], F32))
        sb_cst = top.enter_context(nc.sbuf_tensor("sb_cst", [128, 2], F32))

        with nc.Block("preload") as blk:
            @blk.sync
            def _(e):
                for dst, src in ((sb_id, ap_ident),
                                 (sb_bq, ap_bq), (sb_bk, ap_bk)):
                    cIN.dma(e.dma_start(out=dst[:, :], in_=src))
                e.wait_ge(cIN.sem, cIN.v)

            @blk.vector
            def _(v):
                v.wait_ge(cIN.sem, cIN.v)
                v.memset(sb_cst[:, 0:1], EPS)
                v.memset(sb_cst[:, 1:2], RS)
                cVE.inc(v.tensor_scalar(sb_bqs[:, :], sb_bq[:, :], RS, None, ALU.mult))

        def ln_transpose(ap_src, sb_xt):
            with ExitStack() as ph, block("lnt") as blk:
                b_x = [sbuf(ph, f"lx{i}", [128, D], F32) for i in range(2)]
                b_sq = sbuf(ph, "lsq", [128, D], F32)
                b_xn = [sbuf(ph, f"lxn{i}", [128, D], BF16) for i in range(2)]
                st_sum = sbuf(ph, "lsum", [128, 1], F32)
                st_ssq = sbuf(ph, "lssq", [128, 1], F32)
                st_m = sbuf(ph, "lm", [128, 1], F32)
                st_t1 = sbuf(ph, "lt1", [128, 1], F32)
                st_t2 = sbuf(ph, "lt2", [128, 1], F32)
                st_rstd = sbuf(ph, "lrstd", [128, 1], F32)
                st_nmr = sbuf(ph, "lnmr", [128, 1], F32)
                p_tr = [psum(ph, f"ltr{i}", [128, 128], BF16) for i in range(4)]

                ac0, ve0, pe0 = cAC.v, cVE.v, cPE.v
                xb = [cX[0].v, cX[1].v]
                AC_T = 2
                VE_T = 1 + 16
                PE_T = 16
                _xw = lambda i: xb[i % 2] + 16 * (i // 2 + 1)

                @blk.sync
                def _(e):
                    wait_outs(e)
                    for i in range(16):
                        if i >= 2:
                            e.wait_ge(cAC.sem, ac0 + (i - 2) * AC_T + 2)
                            e.wait_ge(cVE.sem, ve0 + (i - 2) * VE_T + 1)
                        cX[i % 2].dma(e.dma_start(out=b_x[i % 2][:, :], in_=ap_src[i * 128:(i + 1) * 128, :]))

                @blk.scalar
                def _(sc):
                    for i in range(16):
                        sc.wait_ge(cX[i % 2].sem, _xw(i))
                        if i >= 1:
                            sc.wait_ge(cVE.sem, ve0 + (i - 1) * VE_T + 1)
                        cAC.inc(sc.activation(b_sq[:, :], b_x[i % 2][:, :], AF.Square))
                        sc.wait_ge(cVE.sem, ve0 + i * VE_T + 1)
                        sc.activation(st_m[:, :], st_sum[:, :], AF.Identity, scale=1.0 / D)
                        sc.activation(st_t1[:, :], st_m[:, :], AF.Square)
                        sc.activation(st_t1[:, :], st_t1[:, :], AF.Identity, scale=-1.0, bias=sb_cst[:, 0:1])
                        sc.activation(st_t2[:, :], st_m[:, :], AF.Identity, scale=-1.0)
                        sc.activation(st_t1[:, :], st_ssq[:, :], AF.Ln, scale=1.0 / D, bias=st_t1[:, :])
                        sc.activation(st_rstd[:, :], st_t1[:, :], AF.Exp, scale=-0.5)
                        sc.activation(st_t1[:, :], st_t1[:, :], AF.Identity)
                        sc.activation(st_nmr[:, :], st_t2[:, :], AF.Identity, scale=st_rstd[:, :])
                        sc.activation(st_t1[:, :], st_t1[:, :], AF.Identity)
                        if i >= 2:
                            sc.wait_ge(cPE.sem, pe0 + (i - 1) * PE_T)
                        cAC.inc(sc.activation(b_xn[i % 2][:, :], b_x[i % 2][:, :], AF.Identity,
                                              scale=st_rstd[:, :], bias=st_nmr[:, :]))

                @blk.vector
                def _(v):
                    for i in range(16):
                        v.wait_ge(cX[i % 2].sem, _xw(i))
                        if i >= 1:
                            v.wait_ge(cAC.sem, ac0 + (i - 1) * AC_T + 2)
                        v.reduce_sum(st_sum[:, :], b_x[i % 2][:, :], AX.X)
                        v.wait_ge(cAC.sem, ac0 + i * AC_T + 1)
                        cVE.inc(v.reduce_sum(st_ssq[:, :], b_sq[:, :], AX.X))
                        for c in range(16):
                            v.wait_ge(cPE.sem, pe0 + i * PE_T + c + 1)
                            cVE.inc(v.tensor_copy(sb_xt[c][:, i * 128:(i + 1) * 128], p_tr[c % 4][:, :]))

                @blk.tensor
                def _(t):
                    for i in range(16):
                        t.wait_ge(cAC.sem, ac0 + i * AC_T + 2)
                        for c in range(16):
                            k = i * 16 + c
                            if k >= 4:
                                i4, c4 = (k - 4) // 16, (k - 4) % 16
                                t.wait_ge(cVE.sem, ve0 + i4 * VE_T + 1 + c4 + 1)
                            cPE.inc(t.transpose(p_tr[c % 4][:, :], b_xn[i % 2][:, 128 * c:128 * (c + 1)], sb_id[:, :]))

                assert cX[0].v == xb[0] + 128 and cX[1].v == xb[1] + 128
                assert cAC.v == ac0 + 16 * AC_T
                assert cVE.v == ve0 + 16 * VE_T
                assert cPE.v == pe0 + 16 * PE_T

        def emit_decode(v, b_src_i16, b_s1, b_s2, b_s3, b_s4, plane, wdst_ap):
            if plane == 0:
                v.tensor_scalar(b_s1, b_src_i16, 7, None, ALU.bitwise_and)
                v.tensor_scalar(b_s2, b_src_i16, 8, 2, ALU.bitwise_and, ALU.logical_shift_right)
            else:
                v.tensor_scalar(b_s1, b_src_i16, 4, 7, ALU.logical_shift_right, ALU.bitwise_and)
                v.tensor_scalar(b_s2, b_src_i16, 6, 2, ALU.logical_shift_right, ALU.bitwise_and)
            v.tensor_scalar(b_s3, b_s1, 2, -4, ALU.mult, ALU.add)
            v.tensor_tensor(b_s4, b_s1, b_s3, ALU.max)
            v.tensor_scalar(b_s3, b_s4, 2, -8, ALU.mult, ALU.add)
            v.tensor_tensor(b_s1, b_s4, b_s3, ALU.max)
            v.tensor_scalar(b_s2, b_s2, -1, 1, ALU.mult, ALU.add)
            v.tensor_tensor(b_s4, b_s1, b_s2, ALU.mult)

        def ffn(ap_src, ap_dst, dump_xt=False, skip_c=False):
            with ExitStack() as sub:
                sb_xt = [sbuf(sub, f"xt{c}", [128, S], BF16) for c in range(16)]
                ln_transpose(ap_src, sb_xt)
                if dump_xt:
                    with block("xtd") as blk:
                        @blk.gpsimd
                        def _(g_):
                            for c in range(16):
                                cOUT.dma(g_.dma_start(out=ap_xtd[128 * c:128 * (c + 1), :], in_=sb_xt[c][:, :]))
                            g_.wait_ge(cOUT.sem, cOUT.v)

                with ExitStack() as ph, block("ffnb") as blk:
                    b_pk = [sbuf(ph, f"fpk{m}", [128, 4096], U8) for m in range(2)]
                    b_sxs = [sbuf(ph, f"fsxs{i}", [128, 1024], F32) for i in range(2)]
                    b_sxu = sbuf(ph, "fsxu", [128, 4096], BF16)
                    b_sxg = sbuf(ph, "fsxg", [128, 4096], BF16)
                    b_b0 = sbuf(ph, "fb0", [128, 1024], I16)
                    b_s1 = sbuf(ph, "fs1", [128, 1024], I16)
                    b_s2 = sbuf(ph, "fs2", [128, 1024], I16)
                    b_s3 = sbuf(ph, "fs3", [128, 1024], I16)
                    b_s4 = sbuf(ph, "fs4", [128, 1024], I16)
                    b_wg = [[sbuf(ph, f"fwg{pp}{p}", [128, 4096], BF16) for p in range(2)] for pp in range(2)]
                    b_wu = [[sbuf(ph, f"fwu{pp}{p}", [128, 4096], BF16) for p in range(2)] for pp in range(2)]
                    b_gs = [sbuf(ph, f"fgs{i}", [128, 512], F32) for i in range(2)]
                    b_g2 = [sbuf(ph, f"fg2{i}", [128, 512], F32) for i in range(2)]
                    b_u1 = sbuf(ph, "fu1", [128, 512], F32)
                    b_v1 = sbuf(ph, "fv1", [128, 512], F32)
                    b_is = [sbuf(ph, f"fis{i}", [128, S], BF16) for i in range(2)]
                    p_g = [psum(ph, f"fpg{i}", [128, 512], F32) for i in range(3)]
                    p_u = [psum(ph, f"fpu{i}", [128, 512], F32) for i in range(3)]

                    ac0, ve0, pe0 = cAC.v, cVE.v, cPE.v
                    pkg0, pku0 = cPKG.v, cPKU.v
                    sxf0 = [cSXF[0].v, cSXF[1].v]
                    isb = [cIS[0].v, cIS[1].v]
                    NS = 32

                    ac_ms, ve_ms, pe_ms = {}, {}, {}
                    a = ac0
                    for qt_ in range(4):
                        a += 1; ac_ms[("cxu", 0, qt_)] = a
                    for qt_ in range(4):
                        a += 1; ac_ms[("cxg", 0, qt_)] = a
                    for s in range(NS):
                        for i in range(16):
                            if i % 2 == 0 and s + 1 < NS:
                                a += 1
                                ac_ms[("cxu", s + 1, i // 2) if i < 8 else ("cxg", s + 1, (i - 8) // 2)] = a
                            a += 1; ac_ms[("sq", s, i)] = a
                            a += 1; ac_ms[("th", s, i)] = a
                    ac_end = a
                    v_ = ve0
                    for mat in range(2):
                        for j in range(8):
                            if j == 6:
                                v_ += 1; ve_ms[("cu8" if mat == 0 else "cg8", 0)] = v_
                            v_ += 1; ve_ms[("uch" if mat == 0 else "gch", 0, j)] = v_
                    for s in range(NS):
                        for i in range(16):
                            v_ += 1; ve_ms[("u", s, i)] = v_
                            v_ += 1; ve_ms[("it", s, i)] = v_
                            if s + 1 < NS:
                                j = i if i < 8 else i - 8
                                if j == 6:
                                    v_ += 1; ve_ms[("cu8" if i < 8 else "cg8", s + 1)] = v_
                                v_ += 1; ve_ms[("uch" if i < 8 else "gch", s + 1, j)] = v_
                    ve_end = v_
                    p_ = pe0
                    for s in range(NS):
                        for i in range(16):
                            p_ += 1; pe_ms[("g", s, i)] = p_
                            p_ += 1; pe_ms[("up", s, i)] = p_
                    pe_end = p_

                    def slot_key(q):
                        s_, r_ = q // 8, q % 8
                        return ("cxu", s_, r_) if r_ < 4 else ("cxg", s_, r_ - 4)

                    @blk.sync
                    def _(e):
                        wait_outs(e)
                        for m in range(NS):
                            if m >= 1:
                                e.wait_ge(cVE.sem, ve_ms[("cu8", m - 1)])
                            cPKU.dma(e.dma_start(out=b_pk[1][:, :], in_=ap_pku[128 * m:128 * (m + 1), :]))
                            for r in range(4):
                                q = 8 * m + r
                                if q >= 2:
                                    e.wait_ge(cAC.sem, ac_ms[slot_key(q - 2)])
                                cSXF[q % 2].dma(e.dma_start(
                                    out=b_sxs[q % 2][:, :],
                                    in_=ap_scu[128 * m:128 * (m + 1), 1024 * r:1024 * (r + 1)]))
                            if m >= 1:
                                e.wait_ge(cVE.sem, ve_ms[("cg8", m - 1)])
                            cPKG.dma(e.dma_start(out=b_pk[0][:, :], in_=ap_pkg[128 * m:128 * (m + 1), :]))
                            for r in range(4):
                                q = 8 * m + 4 + r
                                if q >= 2:
                                    e.wait_ge(cAC.sem, ac_ms[slot_key(q - 2)])
                                cSXF[q % 2].dma(e.dma_start(
                                    out=b_sxs[q % 2][:, :],
                                    in_=ap_scg[128 * m:128 * (m + 1), 1024 * r:1024 * (r + 1)]))

                    @blk.scalar
                    def _(sc):
                        def castx(s, mat, qt_):
                            q = 8 * s + 4 * mat + qt_
                            sc.wait_ge(cSXF[q % 2].sem, sxf0[q % 2] + 16 * (q // 2 + 1))
                            if s >= 1:
                                sc.wait_ge(cVE.sem, ve_ms[("uch" if mat == 0 else "gch", s - 1, 2 * qt_ + 1)])
                            dst = b_sxu if mat == 0 else b_sxg
                            cAC.inc(sc.activation(dst[:, 1024 * qt_:1024 * (qt_ + 1)], b_sxs[q % 2][:, :],
                                                  AF.Identity, scale=0.5))

                        for qt_ in range(4):
                            castx(0, 0, qt_)
                        for qt_ in range(4):
                            castx(0, 1, qt_)
                        for s in range(NS):
                            for i in range(16):
                                if i % 2 == 0 and s + 1 < NS:
                                    castx(s + 1, 0 if i < 8 else 1, (i % 8) // 2)
                                sc.wait_ge(cPE.sem, pe_ms[("g", s, i)])
                                if i >= 2:
                                    sc.wait_ge(cVE.sem, ve_ms[("it", s, i - 2)])
                                elif s >= 1:
                                    sc.wait_ge(cVE.sem, ve_ms[("it", s - 1, 14 + i)])
                                sc.activation(b_gs[i % 2][:, :], p_g[(16 * s + i) % 3][:, :], AF.Copy)
                                cAC.inc(sc.activation(b_g2[i % 2][:, :], p_g[(16 * s + i) % 3][:, :], AF.Square))
                                sc.wait_ge(cVE.sem, ve_ms[("u", s, i)])
                                cAC.inc(sc.activation(b_g2[i % 2][:, :], b_u1[:, :], AF.Tanh))

                    @blk.tensor
                    def _(t):
                        for s in range(NS):
                            for i in range(16):
                                nt, tsl = i // 4, i % 4
                                if i == 0:
                                    t.wait_ge(cVE.sem, ve_ms[("gch", s, 7)])
                                if i >= 3:
                                    t.wait_ge(cAC.sem, ac_ms[("sq", s, i - 3)])
                                    t.wait_ge(cVE.sem, ve_ms[("it", s, i - 3)])
                                elif s >= 1:
                                    t.wait_ge(cAC.sem, ac_ms[("sq", s - 1, 13 + i)])
                                    t.wait_ge(cVE.sem, ve_ms[("it", s - 1, 13 + i)])
                                for c in range(16):
                                    w = b_wg[s % 2][0 if c < 8 else 1]
                                    cc = c % 8
                                    inst = t.matmul(p_g[(16 * s + i) % 3][:, :],
                                                    w[:, 512 * cc + 128 * nt:512 * cc + 128 * (nt + 1)],
                                                    sb_xt[c][:, 512 * tsl:512 * (tsl + 1)],
                                                    start=(c == 0), stop=(c == 15))
                                    if c == 15:
                                        cPE.inc(inst)
                                if i == 0:
                                    t.wait_ge(cVE.sem, ve_ms[("uch", s, 7)])
                                for c in range(16):
                                    w = b_wu[s % 2][0 if c < 8 else 1]
                                    cc = c % 8
                                    inst = t.matmul(p_u[(16 * s + i) % 3][:, :],
                                                    w[:, 512 * cc + 128 * nt:512 * cc + 128 * (nt + 1)],
                                                    sb_xt[c][:, 512 * tsl:512 * (tsl + 1)],
                                                    start=(c == 0), stop=(c == 15))
                                    if c == 15:
                                        cPE.inc(inst)

                    @blk.vector
                    def _(v):
                        def chunk(v, s, mat, j):
                            qt_, plane = j // 2, j % 2
                            sl = slice(1024 * qt_, 1024 * (qt_ + 1))
                            if j == 0:
                                if mat == 0:
                                    v.wait_ge(cPKU.sem, pku0 + 16 * (s + 1))
                                    if s >= 2:
                                        v.wait_ge(cPE.sem, pe_ms[("up", s - 2, 15)])
                                else:
                                    v.wait_ge(cPKG.sem, pkg0 + 16 * (s + 1))
                                    if s >= 2:
                                        v.wait_ge(cPE.sem, pe_ms[("g", s - 2, 15)])
                            if plane == 0:
                                inst = v.tensor_copy(b_b0[:, :], b_pk[1 - mat][:, sl])
                                if j == 6:
                                    cVE.inc(inst)
                            emit_decode(v, b_b0[:, :], b_s1[:, :], b_s2[:, :],
                                        b_s3[:, :], b_s4[:, :], plane, None)
                            v.wait_ge(cAC.sem, ac_ms[("cxu" if mat == 0 else "cxg", s, j // 2)])
                            wdst = (b_wu if mat == 0 else b_wg)[s % 2][plane]
                            bsx = b_sxu if mat == 0 else b_sxg
                            cVE.inc(v.tensor_tensor(wdst[:, sl], b_s4[:, :], bsx[:, sl], ALU.mult))

                        for mat in range(2):
                            for j in range(8):
                                chunk(v, 0, mat, j)
                        for s in range(NS):
                            for i in range(16):
                                nt, tsl = i // 4, i % 4
                                gnt = 4 * s + nt
                                v.wait_ge(cAC.sem, ac_ms[("sq", s, i)])
                                v.tensor_scalar(b_v1[:, :], b_g2[i % 2][:, :], GC3, GC1, ALU.mult, ALU.add)
                                cVE.inc(v.tensor_tensor(b_u1[:, :], b_v1[:, :], b_gs[i % 2][:, :], ALU.mult))
                                v.wait_ge(cAC.sem, ac_ms[("th", s, i)])
                                v.tensor_scalar(b_v1[:, :], b_g2[i % 2][:, :], 0.5, 0.5, ALU.mult, ALU.add)
                                v.tensor_tensor(b_v1[:, :], b_v1[:, :], b_gs[i % 2][:, :], ALU.mult)
                                v.wait_ge(cPE.sem, pe_ms[("up", s, i)])
                                if tsl == 0 and gnt >= 2:
                                    v.wait_ge(cIS[gnt % 2].sem, isb[gnt % 2] + 16 * (gnt // 2))
                                cVE.inc(v.tensor_tensor(b_is[gnt % 2][:, 512 * tsl:512 * (tsl + 1)],
                                                        b_v1[:, :], p_u[(16 * s + i) % 3][:, :], ALU.mult))
                                if s + 1 < NS:
                                    chunk(v, s + 1, 0 if i < 8 else 1, i if i < 8 else i - 8)

                    @blk.gpsimd
                    def _(g):
                        for s in range(NS):
                            for nt in range(4):
                                gnt = 4 * s + nt
                                g.wait_ge(cVE.sem, ve_ms[("it", s, nt * 4 + 3)])
                                cIS[gnt % 2].dma(g.dma_start(out=ap_intr[128 * gnt:128 * (gnt + 1), :], in_=b_is[gnt % 2][:, :]))

                    assert cPKG.v == pkg0 + 512 and cPKU.v == pku0 + 512
                    assert cSXF[0].v == sxf0[0] + 16 * 128 and cSXF[1].v == sxf0[1] + 16 * 128
                    assert cAC.v == ac_end, (cAC.v, ac_end)
                    assert cVE.v == ve_end, (cVE.v, ve_end)
                    assert cPE.v == pe_end, (cPE.v, pe_end)
                    assert cIS[0].v == isb[0] + 64 * 16 and cIS[1].v == isb[1] + 64 * 16

            if skip_c:
                return
            with ExitStack() as ph, block("ffnc") as blk:
                b_acc = [sbuf(ph, f"dac{i}", [128, 1024], F32) for i in range(16)]
                b_it = [sbuf(ph, f"dit{i}", [128, S], BF16) for i in range(16)]
                b_pk = sbuf(ph, "dpk", [128, 4096], U8)
                sxs = [sbuf(ph, f"dsxs{i}", [128, 1024], F32) for i in range(2)]
                b_sx = sbuf(ph, "dsx", [128, 4096], BF16)
                b_b16 = sbuf(ph, "db16", [128, 4096], I16)
                b_s1 = sbuf(ph, "ds1", [128, 2048], I16)
                b_s2 = sbuf(ph, "ds2", [128, 2048], I16)
                b_s3 = sbuf(ph, "ds3", [128, 2048], I16)
                b_s4 = sbuf(ph, "ds4", [128, 2048], I16)
                b_wd = [[sbuf(ph, f"dwd{pp}{p}", [128, 4096], BF16) for p in range(2)] for pp in range(2)]
                p_d = [psum(ph, f"dpd{i}", [128, 1024], F32) for i in range(3)]

                ac0, ve0, pe0 = cAC.v, cVE.v, cPE.v
                it0 = [cIT[0].v, cIT[1].v]
                pkd0, acc0, fl0 = cPKD.v, cACC.v, cFL.v
                sc20 = [cSC2[0].v, cSC2[1].v]
                NG = 32

                ac_ms, ve_ms, pe_ms = {}, {}, {}
                a = ac0
                for c in range(4):
                    a += 1; ac_ms[("cxd", 0, c)] = a
                for m in range(NG):
                    if m + 1 < NG:
                        for c in range(4):
                            a += 1; ac_ms[("cxd", m + 1, c)] = a
                ac_end = a
                v_ = ve0
                v_ += 1; ve_ms[("c16", 0)] = v_
                for c in range(4):
                    v_ += 1; ve_ms[("dw", 0, c)] = v_
                for m in range(NG):
                    for tt in range(16):
                        v_ += 1; ve_ms[("ac", m, tt)] = v_
                        if tt % 4 == 1 and m + 1 < NG:
                            c = tt // 4
                            if c == 0:
                                v_ += 1; ve_ms[("c16", m + 1)] = v_
                            v_ += 1; ve_ms[("dw", m + 1, c)] = v_
                ve_end = v_
                p_ = pe0
                for m in range(NG):
                    for tt in range(16):
                        p_ += 1; pe_ms[("mm", m, tt)] = p_
                pe_end = p_

                @blk.sync
                def _(e):
                    wait_outs(e)
                    for m in range(NG):
                        h2, g = m // 16, m % 16
                        if m >= 2:
                            e.wait_ge(cPE.sem, pe_ms[("mm", m - 2, 15)])
                        for k in range(8):
                            j = 4 * g + k if k < 4 else 64 + 4 * g + (k - 4)
                            slot = (m % 2) * 8 + k
                            cIT[m % 2].dma(e.dma_start(out=b_it[slot][:, :], in_=ap_intr[128 * j:128 * (j + 1), :]))
                        if m >= 1:
                            e.wait_ge(cVE.sem, ve_ms[("c16", m - 1)])
                        cPKD.dma(e.dma_start(out=b_pk[:, :],
                                             in_=ap_pkd[256 * g + 128 * h2:256 * g + 128 * (h2 + 1), :]))
                        for c in range(4):
                            q = 4 * m + c
                            if q >= 2:
                                q2 = q - 2
                                e.wait_ge(cAC.sem, ac_ms[("cxd", q2 // 4, q2 % 4)])
                            cSC2[q % 2].dma(e.dma_start(
                                out=sxs[q % 2][:, :],
                                in_=ap_scd[128 * m:128 * (m + 1), 1024 * c:1024 * (c + 1)]))
                        if g == 0:
                            if h2 == 1:
                                e.wait_ge(cFL.sem, fl0 + 16 * 16)
                            for i in range(16):
                                cACC.dma(e.dma_start(out=b_acc[i][:, :],
                                                     in_=ap_src[128 * i:128 * (i + 1), 1024 * h2:1024 * (h2 + 1)]))

                @blk.scalar
                def _(sc):
                    def chain(m):
                        for c in range(4):
                            q = 4 * m + c
                            sc.wait_ge(cSC2[q % 2].sem, sc20[q % 2] + 16 * (q // 2 + 1))
                            if m >= 1:
                                sc.wait_ge(cVE.sem, ve_ms[("dw", m - 1, 2 * (c // 2) + 1)])
                            cAC.inc(sc.activation(b_sx[:, 1024 * c:1024 * (c + 1)], sxs[q % 2][:, :],
                                                  AF.Identity, scale=0.25))

                    chain(0)
                    for m in range(NG):
                        if m + 1 < NG:
                            chain(m + 1)

                @blk.tensor
                def _(t):
                    for m in range(NG):
                        for tt in range(16):
                            if tt == 0:
                                t.wait_ge(cIT[m % 2].sem, it0[m % 2] + 128 * (m // 2 + 1))
                                t.wait_ge(cVE.sem, ve_ms[("dw", m, 3)])
                            if tt >= 3:
                                t.wait_ge(cVE.sem, ve_ms[("ac", m, tt - 3)])
                            elif m >= 1:
                                t.wait_ge(cVE.sem, ve_ms[("ac", m - 1, 13 + tt)])
                            for k in range(8):
                                c4, plane = k % 4, k // 4
                                slot = (m % 2) * 8 + (c4 if plane == 0 else 4 + c4)
                                wsrc = b_wd[m % 2][plane]
                                for j in range(2):
                                    inst = t.matmul(p_d[(16 * m + tt) % 3][:, 512 * j:512 * (j + 1)],
                                                    b_it[slot][:, 128 * tt:128 * (tt + 1)],
                                                    wsrc[:, 1024 * c4 + 512 * j:1024 * c4 + 512 * (j + 1)],
                                                    start=(k == 0), stop=(k == 7))
                                    if k == 7 and j == 1:
                                        cPE.inc(inst)

                @blk.vector
                def _(v):
                    def chunk(v, m, c):
                        half, plane = c // 2, c % 2
                        sl = slice(2048 * half, 2048 * (half + 1))
                        if c == 0:
                            v.wait_ge(cPKD.sem, pkd0 + 16 * (m + 1))
                            if m >= 2:
                                v.wait_ge(cPE.sem, pe_ms[("mm", m - 2, 15)])
                            cVE.inc(v.tensor_copy(b_b16[:, :], b_pk[:, :]))
                        emit_decode(v, b_b16[:, sl], b_s1[:, :], b_s2[:, :], b_s3[:, :], b_s4[:, :], plane, None)
                        v.wait_ge(cAC.sem, ac_ms[("cxd", m, 2 * half + 1)])
                        cVE.inc(v.tensor_tensor(b_wd[m % 2][plane][:, sl], b_s4[:, :], b_sx[:, sl], ALU.mult))

                    for c in range(4):
                        chunk(v, 0, c)
                    for m in range(NG):
                        h2, g = m // 16, m % 16
                        for tt in range(16):
                            v.wait_ge(cPE.sem, pe_ms[("mm", m, tt)])
                            if tt == 0 and g == 0:
                                v.wait_ge(cACC.sem, acc0 + 256 * (h2 + 1))
                            cVE.inc(v.tensor_tensor(b_acc[tt][:, :], b_acc[tt][:, :],
                                                    p_d[(16 * m + tt) % 3][:, :], ALU.add))
                            if tt % 4 == 1 and m + 1 < NG:
                                chunk(v, m + 1, tt // 4)

                @blk.gpsimd
                def _(g_):
                    for h2 in range(2):
                        for i in range(16):
                            g_.wait_ge(cVE.sem, ve_ms[("ac", h2 * 16 + 15, i)])
                            cFL.dma(g_.dma_start(out=ap_dst[128 * i:128 * (i + 1), 1024 * h2:1024 * (h2 + 1)],
                                                 in_=b_acc[i][:, :]))

                assert cIT[0].v == it0[0] + 128 * 16 and cIT[1].v == it0[1] + 128 * 16
                assert cPKD.v == pkd0 + 512
                assert cSC2[0].v == sc20[0] + 16 * 64 and cSC2[1].v == sc20[1] + 16 * 64
                assert cACC.v == acc0 + 512
                assert cAC.v == ac_end, (cAC.v, ac_end)
                assert cVE.v == ve_end, (cVE.v, ve_end)
                assert cPE.v == pe_end, (cPE.v, pe_end)
                assert cFL.v == fl0 + 512

        def attn(ap_src, ap_dst):
            with ExitStack() as sub:
                with ExitStack() as ph:
                    sb_xt = [sbuf(ph, f"axt{c}", [128, S], BF16) for c in range(16)]
                    ln_transpose(ap_src, sb_xt)

                    with ExitStack() as ph2, block("proj") as blk:
                        b_wf = [sbuf(ph2, f"awf{i}", [128, D], F32) for i in range(2)]
                        b_wb = [sbuf(ph2, f"awb{c}", [128, D], BF16) for c in range(16)]
                        b_st = [sbuf(ph2, f"ast{i}", [128, 512], BF16) for i in range(2)]
                        sb_bvb = sbuf(ph2, "sb_bvb", [128, D], F32)
                        p_p = [psum(ph2, f"app{i}", [128, 512], F32) for i in range(2)]

                        ac0, ve0, pe0 = cAC.v, cVE.v, cPE.v
                        wf0 = [cWF[0].v, cWF[1].v]
                        one0 = cONE.v
                        w_src = [ap_wq, ap_wk, ap_wv]
                        o_dst = [ap_qt, ap_kt, None]
                        st_hist = [cST[0].v, cST[1].v]
                        st_base = [cST[0].v, cST[1].v]
                        st_wait = {}
                        for m_ in range(3):
                            for g_ in range(64):
                                st_wait[(m_, g_)] = st_hist[g_ % 2]
                                st_hist[g_ % 2] += 16 * (1 if m_ < 2 else 4)

                        def _castv(m, c):
                            return ve0 + 16 * m + c + 1

                        def _vevac(g):
                            return ve0 + 48 + g + 1

                        @blk.sync
                        def _(e):
                            wait_outs(e)
                            cONE.dma(e.dma_start(out=sb_bvb[:, :], in_=ap_bvb))
                            for m in range(3):
                                for c in range(16):
                                    k = m * 16 + c
                                    if k >= 2:
                                        e.wait_ge(cVE.sem, _castv((k - 2) // 16, (k - 2) % 16))
                                    cWF[k % 2].dma(e.dma_start(out=b_wf[c % 2][:, :],
                                                               in_=w_src[m][128 * c:128 * (c + 1), :]))

                        @blk.vector
                        def _(v):
                            v.wait_ge(cONE.sem, one0 + 16)
                            for m in range(3):
                                for c in range(16):
                                    k = m * 16 + c
                                    v.wait_ge(cWF[k % 2].sem, wf0[k % 2] + 16 * (k // 2 + 1))
                                    if m > 0 and c == 0:
                                        v.wait_ge(cPE.sem, pe0 + m * 64)
                                    cVE.inc(v.tensor_copy(b_wb[c][:, :], b_wf[c % 2][:, :]))
                            for g in range(64):
                                tt_i, j = g // 4, g % 4
                                v.wait_ge(cPE.sem, pe0 + 2 * 64 + g + 1)
                                if st_wait[(2, g)] > st_base[g % 2]:
                                    v.wait_ge(cST[g % 2].sem, st_wait[(2, g)])
                                cVE.inc(v.scalar_tensor_tensor(
                                    out=b_st[g % 2][:, :],
                                    in0=sb_bvb[:, 512 * j:512 * (j + 1)], scalar=1.0,
                                    in1=p_p[g % 2][:, :], op0=ALU.mult, op1=ALU.add))

                        @blk.tensor
                        def _(t):
                            for m in range(3):
                                t.wait_ge(cVE.sem, _castv(m, 15))
                                for g in range(64):
                                    ot, j = g // 4, g % 4
                                    G = m * 64 + g
                                    if G >= 2:
                                        pm, pg2 = divmod(G - 2, 64)
                                        if pm < 2:
                                            t.wait_ge(cAC.sem, ac0 + pm * 64 + pg2 + 1)
                                        else:
                                            t.wait_ge(cVE.sem, _vevac(pg2))
                                    for c in range(16):
                                        if m < 2:
                                            inst = t.matmul(p_p[g % 2][:, :],
                                                            b_wb[c][:, 128 * ot:128 * (ot + 1)],
                                                            sb_xt[c][:, 512 * j:512 * (j + 1)],
                                                            start=(c == 0), stop=(c == 15))
                                        else:
                                            inst = t.matmul(p_p[g % 2][:, :],
                                                            sb_xt[c][:, 128 * ot:128 * (ot + 1)],
                                                            b_wb[c][:, 512 * j:512 * (j + 1)],
                                                            start=(c == 0), stop=(c == 15))
                                        if c == 15:
                                            cPE.inc(inst)

                        @blk.scalar
                        def _(sc):
                            for m in range(2):
                                bias = sb_bqs if m == 0 else sb_bk
                                for g in range(64):
                                    ot = g // 4
                                    sc.wait_ge(cPE.sem, pe0 + m * 64 + g + 1)
                                    if st_wait[(m, g)] > st_base[g % 2]:
                                        sc.wait_ge(cST[g % 2].sem, st_wait[(m, g)])
                                    if m == 0:
                                        cAC.inc(sc.activation(b_st[g % 2][:, :], p_p[g % 2][:, :], AF.Identity,
                                                              bias=bias[:, ot:ot + 1], scale=sb_cst[:, 1:2]))
                                    else:
                                        cAC.inc(sc.activation(b_st[g % 2][:, :], p_p[g % 2][:, :], AF.Identity,
                                                              bias=bias[:, ot:ot + 1]))

                        @blk.gpsimd
                        def _(g_):
                            for m in range(2):
                                dst = o_dst[m]
                                for g in range(64):
                                    ot, j = g // 4, g % 4
                                    g_.wait_ge(cAC.sem, ac0 + m * 64 + g + 1)
                                    cST[g % 2].dma(g_.dma_start(out=dst[128 * ot:128 * (ot + 1), 512 * j:512 * (j + 1)],
                                                                in_=b_st[g % 2][:, :]))
                            for g in range(64):
                                ot, j = g // 4, g % 4
                                g_.wait_ge(cVE.sem, _vevac(g))
                                for hh in range(4):
                                    cST[g % 2].dma(g_.dma_start(
                                        out=ap_vt[(4 * j + hh) * 128:(4 * j + hh + 1) * 128, 128 * ot:128 * (ot + 1)],
                                        in_=b_st[g % 2][:, 128 * hh:128 * (hh + 1)]))

                        assert cWF[0].v == wf0[0] + 24 * 16 and cWF[1].v == wf0[1] + 24 * 16
                        assert cVE.v == ve0 + 48 + 64
                        assert cPE.v == pe0 + 3 * 64
                        assert cAC.v == ac0 + 2 * 64
                        assert cST[0].v == st_hist[0] and cST[1].v == st_hist[1]

                sb_ctx = [sbuf(sub, f"actx{h}", [128, S], BF16) for h in range(NH)]
                with ExitStack() as ph, block("heads") as blk:
                    b_kh = [sbuf(ph, f"akh{i}", [128, S], BF16) for i in range(2)]
                    b_qh = [sbuf(ph, f"aqh{i}", [128, S], BF16) for i in range(2)]
                    b_vh = [sbuf(ph, f"avh{i}", [128, S], BF16) for i in range(2)]
                    b_p3 = [sbuf(ph, f"ap3{i}", [128, S], BF16) for i in range(2)]
                    sb_al = sbuf(ph, "sb_al", [128, S], F32)
                    b_af = sbuf(ph, "aaf", [128, S], F32)
                    b_amdt = sbuf(ph, "aamdt", [128, S], F32)
                    b_ptt = sbuf(ph, "aptt", [128, 16 * 512], BF16)
                    b_sst = sbuf(ph, "asst", [128, 6], F32)
                    sb_tmt = sbuf(ph, "sb_tmt", [128, S], F32)
                    sb_nq = sbuf(ph, "sb_nq", [128, 256], F32)
                    p_s = psum(ph, "aps", [128, 2048], F32)
                    p_tw = [psum(ph, f"aptw{i}", [128, 512], BF16) for i in range(2)]
                    p_av = psum(ph, "apav", [128, 512], F32)

                    ac0, ve0, pe0 = cAC.v, cVE.v, cPE.v
                    kq0 = [cKQ[0].v, cKQ[1].v]
                    one0 = cONE.v
                    pt3 = b_ptt[:, :].rearrange("p (kc u) -> p kc u", u=512)

                    ITERS = [(h, qi) for h in range(NH) for qi in range(16)]
                    prev_of = {ITERS[i]: ITERS[i - 1] for i in range(1, len(ITERS))}
                    prev_of[ITERS[0]] = None
                    UNITS = [(it, prev_of[it]) for it in ITERS] + [(None, ITERS[-1])]

                    pe_sched, ve_sched, ac_sched = {}, {}, {}
                    pe_c, ve_c, ac_c = pe0, ve0, ac0
                    for it, pq in UNITS:
                        if it is not None:
                            nb_ = (it[1] + 4) // 4
                            for b in range(nb_):
                                pe_c += 1
                                pe_sched[("scb",) + it + (b,)] = pe_c
                            pe_sched[("score",) + it] = pe_c
                        if pq is not None:
                            hp, qp = pq
                            for j in range((qp + 4) // 4):
                                pe_c += 1
                                pe_sched[("trg",) + pq + (j,)] = pe_c
                            if qp % 4 == 3:
                                pe_c += 1
                                pe_sched[("av", hp, qp // 4)] = pe_c
                        if it is not None:
                            nb_ = (it[1] + 4) // 4
                            for b in range(nb_ - 1):
                                ve_c += 1
                                ve_sched[("addb",) + it + (b,)] = ve_c
                            ve_c += 1
                            ve_sched[("diag",) + it] = ve_c
                        if pq is not None:
                            hp, qp = pq
                            if qp % 4 == 0:
                                ve_c += 1
                                ve_sched[("ms", hp, qp // 4)] = ve_c
                        if it is not None:
                            ve_c += 1
                            ve_sched[("soft",) + it] = ve_c
                            if 128 * (it[1] + 1) > 512:
                                ac_c += 1
                                ac_sched[("exp1",) + it] = ac_c
                            ac_c += 1
                            ac_sched[("exp",) + it] = ac_c
                        if pq is not None:
                            for j in range((pq[1] + 4) // 4):
                                ac_c += 1
                                ac_sched[("tge",) + pq + (j,)] = ac_c
                        if pq is not None and pq[1] % 4 == 3:
                            ac_c += 1
                            ac_sched[("avev", pq[0], pq[1] // 4)] = ac_c

                    @blk.sync
                    def _(e):
                        wait_outs(e)
                        cONE.dma(e.dma_start(out=sb_tmt[:, :], in_=ap_tmt))
                        cONE.dma(e.dma_start(out=sb_nq[:, :], in_=ap_negafq))
                        cONE.dma(e.dma_start(out=sb_al[:, :], in_=ap_alibi))
                        for h in range(NH):
                            if h >= 2:
                                e.wait_ge(cPE.sem, pe_sched[("av", h - 2, 3)])
                            cKQ[h % 2].dma(e.dma_start(out=b_kh[h % 2][:, :], in_=ap_kt[128 * h:128 * (h + 1), :]))
                            cKQ[h % 2].dma(e.dma_start(out=b_qh[h % 2][:, :], in_=ap_qt[128 * h:128 * (h + 1), :]))
                            cKQ[h % 2].dma(e.dma_start(out=b_vh[h % 2][:, :], in_=ap_vt[128 * h:128 * (h + 1), :]))

                    @blk.tensor
                    def _(t):
                        for it, pq in UNITS:
                            if it is not None:
                                h, qi = it
                                if qi == 0:
                                    t.wait_ge(cKQ[h % 2].sem, kq0[h % 2] + 48 * (h // 2 + 1))
                                klen = 128 * (qi + 1)
                                nb = (klen + 511) // 512
                                nbp = (128 * (pq[1] + 1) + 511) // 512 if pq is not None else 0
                                for b in range(nb):
                                    if pq is not None and b == 0:
                                        t.wait_ge(cAC.sem, ac_sched[("exp1",) + pq] if nbp > 1
                                                  else ac_sched[("exp",) + pq])
                                    if pq is not None and b == nbp - 1:
                                        t.wait_ge(cAC.sem, ac_sched[("exp",) + pq])
                                    w = min(512, klen - 512 * b)
                                    cPE.inc(t.matmul(p_s[:, 512 * b:512 * b + w],
                                                     b_qh[h % 2][:, 128 * qi:128 * (qi + 1)],
                                                     b_kh[h % 2][:, 512 * b:512 * b + w],
                                                     start=True, stop=True))
                            if pq is None:
                                continue
                            hp, qp = pq
                            t.wait_ge(cVE.sem, ve_sched[("soft",) + pq])
                            for kc in range(qp + 1):
                                j = kc // 4
                                gi = _TRG_IDX[pq + (j,)]
                                if kc % 4 == 0 and gi >= 2:
                                    t.wait_ge(cAC.sem, ac_sched[("tge",) + _TRG_SEQ[gi - 2]])
                                inst = t.transpose(p_tw[gi % 2][:, 128 * (kc % 4):128 * (kc % 4 + 1)],
                                                   b_p3[qp % 2][:, 128 * kc:128 * (kc + 1)], sb_id[:, :])
                                if kc % 4 == 3 or kc == qp:
                                    cPE.inc(inst)
                            if qp % 4 == 3:
                                g = qp // 4
                                t.wait_ge(cAC.sem, ac_sched[("tge",) + pq + (qp // 4,)])
                                if (hp, g) != (0, 0):
                                    pg = (hp, g - 1) if g > 0 else (hp - 1, 3)
                                    t.wait_ge(cAC.sem, ac_sched[("avev",) + pg])
                                for kc in range(4 * (g + 1)):
                                    inst = t.matmul(p_av[:, :], b_vh[hp % 2][:, 128 * kc:128 * (kc + 1)],
                                                    b_ptt[:, 512 * kc:512 * (kc + 1)],
                                                    start=(kc == 0), stop=(kc == 4 * g + 3))
                                    if kc == 4 * g + 3:
                                        cPE.inc(inst)

                    @blk.vector
                    def _(v):
                        for it, pq in UNITS:
                            if it is not None:
                                h, qi = it
                                if qi == 0:
                                    if h == 0:
                                        v.wait_ge(cONE.sem, one0 + 48)
                                    slope = SLOPES[h]
                                    v.tensor_scalar(b_af[:, :], sb_al[:, :], slope, None, ALU.mult)
                                    v.tensor_tensor(b_amdt[:, :], b_af[:, :], sb_tmt[:, :], ALU.add)
                                dg = slice(qi * 128, qi * 128 + 128)
                                b_d = (qi * 128) // 512
                                o = qi * 128 - 512 * b_d
                                for b in range(b_d):
                                    v.wait_ge(cPE.sem, pe_sched[("scb",) + it + (b,)])
                                    cVE.inc(v.tensor_tensor(p_s[:, 512 * b:512 * (b + 1)],
                                                            p_s[:, 512 * b:512 * (b + 1)],
                                                            b_af[:, 512 * b:512 * (b + 1)], ALU.add))
                                v.wait_ge(cPE.sem, pe_sched[("score",) + it])
                                if o > 0:
                                    v.tensor_tensor(p_s[:, 512 * b_d:512 * b_d + o],
                                                    p_s[:, 512 * b_d:512 * b_d + o],
                                                    b_af[:, 512 * b_d:512 * b_d + o], ALU.add)
                                cVE.inc(v.tensor_tensor(p_s[:, dg], p_s[:, dg], b_amdt[:, dg], ALU.add))
                            if pq is not None:
                                hp, qp = pq
                                if qp % 4 == 0:
                                    g = qp // 4
                                    if (hp, g) != (0, 0):
                                        pg = (hp, g - 1) if g > 0 else (hp - 1, 3)
                                        v.wait_ge(cPE.sem, pe_sched[("av",) + pg])
                                    if g < 3:
                                        cVE.inc(v.memset(b_ptt[:, 512 * (4 * g + 1):512 * (4 * g + 4)], 0))
                                    else:
                                        cVE.inc(v.memset(b_ptt[:, 512 * 13:512 * 16], 0))
                            if it is not None:
                                h, qi = it
                                klen = 128 * (qi + 1)
                                nb = (klen + 511) // 512
                                v.wait_ge(cAC.sem, ac_sched[("exp",) + it])
                                if nb > 1:
                                    v.tensor_tensor(b_sst[:, 3:4], b_sst[:, 4:5], b_sst[:, 5:6], ALU.add)
                                    v.drain()
                                v.reciprocal(b_sst[:, 2:3], b_sst[:, 3:4])
                                v.drain()
                                cVE.inc(v.tensor_scalar(b_p3[qi % 2][:, 0:klen], b_p3[qi % 2][:, 0:klen],
                                                        b_sst[:, 2:3], None, ALU.mult))

                    @blk.scalar
                    def _(sc):
                        for it, pq in UNITS:
                            if it is not None:
                                h, qi = it
                                klen = 128 * (qi + 1)
                                nb = (klen + 511) // 512
                                bias = sb_nq[:, 16 * h + qi:16 * h + qi + 1]
                                if nb > 1:
                                    sc.wait_ge(cVE.sem, ve_sched[("addb",) + it + (nb - 2,)])
                                    cAC.inc(sc.activation(b_p3[qi % 2][:, 0:512 * (nb - 1)], p_s[:, 0:512 * (nb - 1)],
                                                          AF.Exp, bias=bias, accum_out=b_sst[:, 4:5]))
                                sc.wait_ge(cVE.sem, ve_sched[("diag",) + it])
                                cAC.inc(sc.activation(b_p3[qi % 2][:, 512 * (nb - 1):klen],
                                                      p_s[:, 512 * (nb - 1):klen], AF.Exp, bias=bias,
                                                      accum_out=b_sst[:, 5:6] if nb > 1 else b_sst[:, 3:4]))
                            if pq is not None:
                                hp, qp = pq
                                g4 = qp // 4
                                for j in range((qp + 4) // 4):
                                    gi = _TRG_IDX[pq + (j,)]
                                    sc.wait_ge(cPE.sem, pe_sched[("trg",) + pq + (j,)])
                                    if j == g4 and qp % 4 != 0:
                                        sc.wait_ge(cVE.sem, ve_sched[("ms", hp, g4)])
                                    nk = min(qp, 4 * j + 3) - 4 * j + 1
                                    cAC.inc(sc.activation(
                                        pt3[:, 4 * j:4 * j + nk, 128 * (qp % 4):128 * (qp % 4 + 1)],
                                        p_tw[gi % 2][:, 0:128 * nk].rearrange("p (t u) -> p t u", u=128),
                                        AF.Copy))
                            if pq is not None and pq[1] % 4 == 3:
                                sc.wait_ge(cPE.sem, pe_sched[("av", pq[0], pq[1] // 4)])
                                cAC.inc(sc.activation(sb_ctx[pq[0]][:, 512 * (pq[1] // 4):512 * (pq[1] // 4 + 1)],
                                                      p_av[:, :], AF.Copy))

                    assert cPE.v == pe_c, (cPE.v, pe_c)
                    assert cVE.v == ve_c, (cVE.v, ve_c)
                    assert cAC.v == ac_c, (cAC.v, ac_c)
                    assert cKQ[0].v == kq0[0] + 48 * 8 and cKQ[1].v == kq0[1] + 48 * 8
                with ExitStack() as ph, block("oproj") as blk:
                    b_wf = [sbuf(ph, f"owf{i}", [128, D], F32) for i in range(2)]
                    b_wb = [sbuf(ph, f"owb{c}", [128, D], BF16) for c in range(16)]
                    b_x1 = [sbuf(ph, f"ox1{i}", [128, D], F32) for i in range(2)]
                    b_o = [sbuf(ph, f"oo{i}", [128, D], F32) for i in range(2)]
                    sb_bob = sbuf(ph, "sb_bob", [128, D], F32)
                    p_p = [psum(ph, f"opp{i}", [128, 512], F32) for i in range(2)]

                    ac0, ve0, pe0 = cAC.v, cVE.v, cPE.v
                    wf0 = [cWF[0].v, cWF[1].v]
                    x0 = [cX[0].v, cX[1].v]
                    fo0 = [cFO[0].v, cFO[1].v]
                    one0 = cONE.v
                    _vc = lambda c: ve0 + c + 1
                    _vx = lambda i: ve0 + 16 + 5 * i + 1
                    _va = lambda i, j: ve0 + 16 + 5 * i + 1 + j + 1

                    @blk.sync
                    def _(e):
                        wait_outs(e)
                        cONE.dma(e.dma_start(out=sb_bob[:, :], in_=ap_bob))
                        for c in range(16):
                            if c >= 2:
                                e.wait_ge(cVE.sem, _vc(c - 2))
                            cWF[c % 2].dma(e.dma_start(out=b_wf[c % 2][:, :], in_=ap_wo[128 * c:128 * (c + 1), :]))
                        for i in range(16):
                            if i >= 2:
                                e.wait_ge(cVE.sem, _vx(i - 2))
                            cX[i % 2].dma(e.dma_start(out=b_x1[i % 2][:, :], in_=ap_src[128 * i:(i + 1) * 128, :]))

                    @blk.vector
                    def _(v):
                        v.wait_ge(cONE.sem, one0 + 16)
                        for c in range(16):
                            v.wait_ge(cWF[c % 2].sem, wf0[c % 2] + 16 * (c // 2 + 1))
                            cVE.inc(v.tensor_copy(b_wb[c][:, :], b_wf[c % 2][:, :]))
                        for i in range(16):
                            v.wait_ge(cX[i % 2].sem, x0[i % 2] + 16 * (i // 2 + 1))
                            if i >= 2:
                                v.wait_ge(cFO[i % 2].sem, fo0[i % 2] + 16 * (i // 2))
                            cVE.inc(v.tensor_tensor(b_o[i % 2][:, :], b_x1[i % 2][:, :], sb_bob[:, :], ALU.add))
                            for j in range(4):
                                v.wait_ge(cPE.sem, pe0 + 4 * i + j + 1)
                                cVE.inc(v.tensor_tensor(b_o[i % 2][:, 512 * j:512 * (j + 1)],
                                                        b_o[i % 2][:, 512 * j:512 * (j + 1)],
                                                        p_p[j % 2][:, :], ALU.add))

                    @blk.tensor
                    def _(t):
                        t.wait_ge(cVE.sem, _vc(15))
                        for i in range(16):
                            for j in range(4):
                                g = 4 * i + j
                                if g >= 2:
                                    t.wait_ge(cVE.sem, _va((g - 2) // 4, (g - 2) % 4))
                                for h in range(16):
                                    inst = t.matmul(p_p[j % 2][:, :], sb_ctx[h][:, 128 * i:128 * (i + 1)],
                                                    b_wb[h][:, 512 * j:512 * (j + 1)],
                                                    start=(h == 0), stop=(h == 15))
                                    if h == 15:
                                        cPE.inc(inst)

                    @blk.gpsimd
                    def _(g_):
                        for i in range(16):
                            g_.wait_ge(cVE.sem, _va(i, 3))
                            cFO[i % 2].dma(g_.dma_start(out=ap_dst[128 * i:128 * (i + 1), :], in_=b_o[i % 2][:, :]))

                    assert cWF[0].v == wf0[0] + 8 * 16 and cWF[1].v == wf0[1] + 8 * 16
                    assert cX[0].v == x0[0] + 8 * 16 and cX[1].v == x0[1] + 8 * 16
                    assert cVE.v == ve0 + 16 + 16 * 5
                    assert cPE.v == pe0 + 64
                    assert cFO[0].v == fo0[0] + 8 * 16 and cFO[1].v == fo0[1] + 8 * 16

        ffn(ap_x, ap_x1, dump_xt="xtd" in debug_out, skip_c=(stop_after == "ffnb"))
        if stop_after not in ("ffn1", "ffnb"):
            attn(ap_x1, ap_x2)
            ffn(ap_x2, ap_out)

        with nc.Block("fin") as blk:
            @blk.gpsimd
            def _(g_):
                for c_ in OUT_CTRS:
                    if c_.v:
                        g_.wait_ge(c_.sem, c_.v)

    return nc



_DPERM = np.concatenate([np.arange(0, D, 2), np.arange(1, D, 2)])
_NPERM = np.concatenate([np.arange(0, NM, 2), np.arange(1, NM, 2)])
_INV_DPERM = np.argsort(_DPERM)


def _bytes_of_packed(arr_i32, n_rows, n_half):
    a = np.ascontiguousarray(np.asarray(arr_i32, dtype=np.int32))
    return a.view(np.uint8).reshape(n_rows, n_half * 4)[:, ::4]


def host_prep(inputs):
    x = np.asarray(inputs["L0_attn_x"], np.float32)
    wq = np.ascontiguousarray(np.asarray(inputs["L0_attn_Wq"], np.float32)[_DPERM, :])
    wk = np.ascontiguousarray(np.asarray(inputs["L0_attn_Wk"], np.float32)[_DPERM, :])
    wv = np.ascontiguousarray(np.asarray(inputs["L0_attn_Wv"], np.float32)[_DPERM, :])
    wo = np.ascontiguousarray(np.asarray(inputs["L0_attn_Wo"], np.float32)[:, _DPERM])
    bq = np.ascontiguousarray(np.asarray(inputs["L0_attn_bq"], np.float32).reshape(16, 128).T)
    bk = np.ascontiguousarray(np.asarray(inputs["L0_attn_bk"], np.float32).reshape(16, 128).T)
    bvb = np.ascontiguousarray(np.broadcast_to(np.asarray(inputs["L0_attn_bv"], np.float32), (128, D)))
    bob = np.ascontiguousarray(np.broadcast_to(np.asarray(inputs["L0_attn_bo"], np.float32)[_DPERM], (128, D)))

    pkg = _bytes_of_packed(inputs["L0_ffn_gate_W_packed"], NM, 1024)
    pku = _bytes_of_packed(inputs["L0_ffn_up_W_packed"], NM, 1024)
    pkd = _bytes_of_packed(inputs["L0_ffn_down_W_packed"], D, 8192)
    pkgT = pkg.T[:, _NPERM]
    pkuT = pku.T[:, _NPERM]
    pkdT = pkd.T[:, _DPERM]
    pkgR = np.ascontiguousarray(pkgT.reshape(8, 128, 32, 512).transpose(2, 1, 0, 3).reshape(4096, 4096))
    pkuR = np.ascontiguousarray(pkuT.reshape(8, 128, 32, 512).transpose(2, 1, 0, 3).reshape(4096, 4096))
    pkdR = np.ascontiguousarray(pkdT.reshape(16, 4, 128, 2, 1024).transpose(0, 3, 2, 1, 4).reshape(4096, 4096))

    sg = np.asarray(inputs["L0_ffn_gate_scales"], np.float32)[_NPERM, :].T
    su = np.asarray(inputs["L0_ffn_up_scales"], np.float32)[_NPERM, :].T
    sd = np.asarray(inputs["L0_ffn_down_scales"], np.float32)[_DPERM, :].T
    scg = np.ascontiguousarray(np.repeat(
        sg.reshape(8, 8, 32, 512).transpose(1, 2, 0, 3).reshape(8, 32, 4096).transpose(1, 0, 2),
        16, axis=1).reshape(4096, 4096))
    scu = np.ascontiguousarray(np.repeat(
        su.reshape(8, 8, 32, 512).transpose(1, 2, 0, 3).reshape(8, 32, 4096).transpose(1, 0, 2),
        16, axis=1).reshape(4096, 4096))
    scd = np.ascontiguousarray(np.repeat(
        sd.reshape(64, 8, D).transpose(1, 0, 2).reshape(8, 64, 2, 1024)
        .transpose(2, 1, 0, 3).reshape(2, 16, 4, 8, 1024).transpose(0, 1, 3, 2, 4),
        16, axis=2).reshape(4096, 4096))

    ident = np.eye(128, dtype=ml_dtypes.bfloat16)
    qv, kv = np.meshgrid(np.arange(128), np.arange(128), indexing="ij")
    trimask = np.where(kv <= qv, 0.0, NEG).astype(np.float32)
    dist = (np.arange(S) - (S - 1)).astype(np.float32)
    tmt = np.ascontiguousarray(np.tile(trimask, (1, S // 128)))
    alibi = np.ascontiguousarray(np.broadcast_to(dist, (128, S)))
    pgrid = np.arange(128).reshape(128, 1)
    qigrid = np.arange(16).reshape(1, 16)
    negafq = np.zeros((128, 256), np.float32)
    for h in range(NH):
        negafq[:, 16 * h:16 * (h + 1)] = -SLOPES[h] * (128 * qigrid + pgrid - (S - 1))

    shared = dict(wq=wq, wk=wk, wv=wv, wo=wo, bq=bq, bk=bk, bvb=bvb, bob=bob,
                  pkg=pkgR, pku=pkuR, pkd=pkdR, scg=scg, scu=scu, scd=scd,
                  ident=ident, tmt=tmt,
                  negafq=negafq, alibi=alibi)
    xs = [np.ascontiguousarray(x[b][:, _DPERM]) for b in range(NCORE)]
    return shared, xs


def kernel(**inputs):
    shared, xs = host_prep(inputs)
    nc = build()
    in_maps = [dict(shared, x=xs[b]) for b in range(NCORE)]
    res = run_bass_kernel_spmd(nc, in_maps, core_ids=list(range(NCORE)))
    out = np.stack([np.asarray(r["out"], np.float32) for r in res.results])
    return np.ascontiguousarray(out[:, :, _INV_DPERM])


# version 18
# speedup vs baseline: 1.0368x; 1.0121x over previous
from contextlib import ExitStack

import numpy as np
import ml_dtypes

import concourse.bass as bass
import concourse.mybir as mybir
from concourse.bass_utils import run_bass_kernel_spmd

F32 = mybir.dt.float32
BF16 = mybir.dt.bfloat16
I16 = mybir.dt.int16
U8 = mybir.dt.uint8
AF = mybir.ActivationFunctionType
ALU = mybir.AluOpType
AX = mybir.AxisListType

D = 2048
S = 2048
NH = 16
HD = 128
NM = 16384
NCORE = 8

NEG = -3.0e38
RS = 0.08838834764831845
GC1 = 0.7978845608028654
GC3 = GC1 * 0.044715
EPS = 1e-5

SLOPES = [2.0 ** (-0.5 * (h + 1)) for h in range(NH)]

_TRG_SEQ = [(hh, qq, jj) for hh in range(NH) for qq in range(16) for jj in range((qq + 4) // 4)]
_TRG_IDX = {t: i for i, t in enumerate(_TRG_SEQ)}


class Ctr:
    def __init__(self, sem):
        self.sem = sem
        self.v = 0

    def inc(self, inst, n=1):
        inst.then_inc(self.sem, n)
        self.v += n
        return self.v

    def dma(self, inst):
        return self.inc(inst, 16)


def build(debug_out=(), stop_after=None):
    nc = bass.Bass("TRN2", target_bir_lowering=False)

    ap_x = nc.dram_tensor("x", [S, D], F32, kind="ExternalInput").ap()
    ap_wq = nc.dram_tensor("wq", [D, D], F32, kind="ExternalInput").ap()
    ap_wk = nc.dram_tensor("wk", [D, D], F32, kind="ExternalInput").ap()
    ap_wv = nc.dram_tensor("wv", [D, D], F32, kind="ExternalInput").ap()
    ap_wo = nc.dram_tensor("wo", [D, D], F32, kind="ExternalInput").ap()
    ap_bq = nc.dram_tensor("bq", [128, 16], F32, kind="ExternalInput").ap()
    ap_bk = nc.dram_tensor("bk", [128, 16], F32, kind="ExternalInput").ap()
    ap_bvb = nc.dram_tensor("bvb", [128, D], F32, kind="ExternalInput").ap()
    ap_bob = nc.dram_tensor("bob", [128, D], F32, kind="ExternalInput").ap()
    ap_pkg = nc.dram_tensor("pkg", [4096, 4096], U8, kind="ExternalInput").ap()
    ap_pku = nc.dram_tensor("pku", [4096, 4096], U8, kind="ExternalInput").ap()
    ap_pkd = nc.dram_tensor("pkd", [4096, 4096], U8, kind="ExternalInput").ap()
    ap_scg = nc.dram_tensor("scg", [4096, 4096], F32, kind="ExternalInput").ap()
    ap_scu = nc.dram_tensor("scu", [4096, 4096], F32, kind="ExternalInput").ap()
    ap_scd = nc.dram_tensor("scd", [4096, 4096], F32, kind="ExternalInput").ap()
    ap_ident = nc.dram_tensor("ident", [128, 128], BF16, kind="ExternalInput").ap()
    ap_tmt = nc.dram_tensor("tmt", [128, S], F32, kind="ExternalInput").ap()
    ap_negafq = nc.dram_tensor("negafq", [128, 256], F32, kind="ExternalInput").ap()
    ap_alibi = nc.dram_tensor("alibi", [128, S], F32, kind="ExternalInput").ap()

    ap_out = nc.dram_tensor("out", [S, D], F32, kind="ExternalOutput").ap()
    kind_dbg = lambda name: "ExternalOutput" if name in debug_out else "Internal"
    ap_x1 = nc.dram_tensor("x1", [S, D], F32, kind=kind_dbg("x1")).ap()
    ap_x2 = nc.dram_tensor("x2", [S, D], F32, kind=kind_dbg("x2")).ap()
    ap_intr = nc.dram_tensor("intr", [NM, S], BF16, kind=kind_dbg("intr")).ap()
    ap_qt = nc.dram_tensor("qt", [D, S], BF16, kind=kind_dbg("qt")).ap()
    ap_kt = nc.dram_tensor("kt", [D, S], BF16, kind=kind_dbg("kt")).ap()
    ap_vt = nc.dram_tensor("vt", [S, D], BF16, kind=kind_dbg("vt")).ap()
    ap_xtd = (nc.dram_tensor("xtd", [D, S], BF16, kind="ExternalOutput").ap()
              if "xtd" in debug_out else None)
    ap_w0d = (nc.dram_tensor("w0d", [128, 4 * 4096], BF16, kind="ExternalOutput").ap()
              if "w0d" in debug_out else None)
    ap_b16d = (nc.dram_tensor("b16d", [128, 4096], I16, kind="ExternalOutput").ap()
               if "w0d" in debug_out else None)

    with ExitStack() as top:
        def mkctr(name):
            return Ctr(top.enter_context(nc.semaphore(name)))

        cIN = mkctr("cIN")
        cOUT = mkctr("cOUT")
        cPE = mkctr("cPE")
        cVE = mkctr("cVE")
        cAC = mkctr("cAC")
        cX = [mkctr("cXA"), mkctr("cXB")]
        cPKG, cPKU = mkctr("cPKG"), mkctr("cPKU")
        cIT = [mkctr("cITA"), mkctr("cITB")]
        cPKD, cSCD, cACC = mkctr("cPKD"), mkctr("cSCD"), mkctr("cACC")
        cSC2 = [mkctr("cSC2A"), mkctr("cSC2B")]
        cSXF = [mkctr("cSXFA"), mkctr("cSXFB")]
        cWF = [mkctr("cWFA"), mkctr("cWFB")]
        cKQ = [mkctr("cKQA"), mkctr("cKQB")]
        cONE = mkctr("cONE")
        cIS = [mkctr("cISA"), mkctr("cISB")]
        cST = [mkctr("cSTA"), mkctr("cSTB")]
        cFL = mkctr("cFL")
        cFO = [mkctr("cFOA"), mkctr("cFOB")]
        OUT_CTRS = [cOUT, cIS[0], cIS[1], cST[0], cST[1], cFL, cFO[0], cFO[1]]

        def wait_outs(e):
            for c_ in OUT_CTRS:
                if c_.v:
                    e.wait_ge(c_.sem, c_.v)

        _uq = [0]

        def sbuf(scope, name, shape, dt):
            _uq[0] += 1
            return scope.enter_context(nc.sbuf_tensor(f"{name}_{_uq[0]}", shape, dt))

        def psum(scope, name, shape, dt):
            _uq[0] += 1
            return scope.enter_context(nc.psum_tensor(f"{name}_{_uq[0]}", shape, dt))

        def block(name):
            _uq[0] += 1
            return nc.Block(f"{name}_{_uq[0]}")

        sb_id = top.enter_context(nc.sbuf_tensor("sb_id", [128, 128], BF16))
        sb_bq = top.enter_context(nc.sbuf_tensor("sb_bq", [128, 16], F32))
        sb_bk = top.enter_context(nc.sbuf_tensor("sb_bk", [128, 16], F32))
        sb_bqs = top.enter_context(nc.sbuf_tensor("sb_bqs", [128, 16], F32))
        sb_cst = top.enter_context(nc.sbuf_tensor("sb_cst", [128, 4], F32))

        with nc.Block("preload") as blk:
            @blk.sync
            def _(e):
                for dst, src in ((sb_id, ap_ident),
                                 (sb_bq, ap_bq), (sb_bk, ap_bk)):
                    cIN.dma(e.dma_start(out=dst[:, :], in_=src))
                e.wait_ge(cIN.sem, cIN.v)

            @blk.vector
            def _(v):
                v.wait_ge(cIN.sem, cIN.v)
                v.memset(sb_cst[:, 0:1], EPS)
                v.memset(sb_cst[:, 2:3], GC1)
                v.memset(sb_cst[:, 3:4], 0.5)
                v.memset(sb_cst[:, 1:2], RS)
                cVE.inc(v.tensor_scalar(sb_bqs[:, :], sb_bq[:, :], RS, None, ALU.mult))

        def ln_transpose(ap_src, sb_xt):
            with ExitStack() as ph, block("lnt") as blk:
                b_x = [sbuf(ph, f"lx{i}", [128, D], F32) for i in range(2)]
                b_sq = sbuf(ph, "lsq", [128, D], F32)
                b_xn = [sbuf(ph, f"lxn{i}", [128, D], BF16) for i in range(2)]
                st_sum = sbuf(ph, "lsum", [128, 1], F32)
                st_ssq = sbuf(ph, "lssq", [128, 1], F32)
                st_m = sbuf(ph, "lm", [128, 1], F32)
                st_t1 = sbuf(ph, "lt1", [128, 1], F32)
                st_t2 = sbuf(ph, "lt2", [128, 1], F32)
                st_rstd = sbuf(ph, "lrstd", [128, 1], F32)
                st_nmr = sbuf(ph, "lnmr", [128, 1], F32)
                p_tr = [psum(ph, f"ltr{i}", [128, 128], BF16) for i in range(4)]

                ac0, ve0, pe0 = cAC.v, cVE.v, cPE.v
                xb = [cX[0].v, cX[1].v]
                AC_T = 2
                VE_T = 1 + 16
                PE_T = 16
                _xw = lambda i: xb[i % 2] + 16 * (i // 2 + 1)

                @blk.sync
                def _(e):
                    wait_outs(e)
                    for i in range(16):
                        if i >= 2:
                            e.wait_ge(cAC.sem, ac0 + (i - 2) * AC_T + 2)
                            e.wait_ge(cVE.sem, ve0 + (i - 2) * VE_T + 1)
                        cX[i % 2].dma(e.dma_start(out=b_x[i % 2][:, :], in_=ap_src[i * 128:(i + 1) * 128, :]))

                @blk.scalar
                def _(sc):
                    for i in range(16):
                        sc.wait_ge(cX[i % 2].sem, _xw(i))
                        if i >= 1:
                            sc.wait_ge(cVE.sem, ve0 + (i - 1) * VE_T + 1)
                        cAC.inc(sc.activation(b_sq[:, :], b_x[i % 2][:, :], AF.Square))
                        sc.wait_ge(cVE.sem, ve0 + i * VE_T + 1)
                        sc.activation(st_m[:, :], st_sum[:, :], AF.Identity, scale=1.0 / D)
                        sc.activation(st_t1[:, :], st_m[:, :], AF.Square)
                        sc.activation(st_t1[:, :], st_t1[:, :], AF.Identity, scale=-1.0, bias=sb_cst[:, 0:1])
                        sc.activation(st_t2[:, :], st_m[:, :], AF.Identity, scale=-1.0)
                        sc.activation(st_t1[:, :], st_ssq[:, :], AF.Ln, scale=1.0 / D, bias=st_t1[:, :])
                        sc.activation(st_rstd[:, :], st_t1[:, :], AF.Exp, scale=-0.5)
                        sc.activation(st_t1[:, :], st_t1[:, :], AF.Identity)
                        sc.activation(st_nmr[:, :], st_t2[:, :], AF.Identity, scale=st_rstd[:, :])
                        sc.activation(st_t1[:, :], st_t1[:, :], AF.Identity)
                        if i >= 2:
                            sc.wait_ge(cPE.sem, pe0 + (i - 1) * PE_T)
                        cAC.inc(sc.activation(b_xn[i % 2][:, :], b_x[i % 2][:, :], AF.Identity,
                                              scale=st_rstd[:, :], bias=st_nmr[:, :]))

                @blk.vector
                def _(v):
                    for i in range(16):
                        v.wait_ge(cX[i % 2].sem, _xw(i))
                        if i >= 1:
                            v.wait_ge(cAC.sem, ac0 + (i - 1) * AC_T + 2)
                        v.reduce_sum(st_sum[:, :], b_x[i % 2][:, :], AX.X)
                        v.wait_ge(cAC.sem, ac0 + i * AC_T + 1)
                        cVE.inc(v.reduce_sum(st_ssq[:, :], b_sq[:, :], AX.X))
                        for c in range(16):
                            v.wait_ge(cPE.sem, pe0 + i * PE_T + c + 1)
                            cVE.inc(v.tensor_copy(sb_xt[c][:, i * 128:(i + 1) * 128], p_tr[c % 4][:, :]))

                @blk.tensor
                def _(t):
                    for i in range(16):
                        t.wait_ge(cAC.sem, ac0 + i * AC_T + 2)
                        for c in range(16):
                            k = i * 16 + c
                            if k >= 4:
                                i4, c4 = (k - 4) // 16, (k - 4) % 16
                                t.wait_ge(cVE.sem, ve0 + i4 * VE_T + 1 + c4 + 1)
                            cPE.inc(t.transpose(p_tr[c % 4][:, :], b_xn[i % 2][:, 128 * c:128 * (c + 1)], sb_id[:, :]))

                assert cX[0].v == xb[0] + 128 and cX[1].v == xb[1] + 128
                assert cAC.v == ac0 + 16 * AC_T
                assert cVE.v == ve0 + 16 * VE_T
                assert cPE.v == pe0 + 16 * PE_T

        def emit_decode(v, b_src_i16, b_s1, b_s2, b_s3, b_s4, plane, wdst_ap):
            if plane == 0:
                v.tensor_scalar(b_s1, b_src_i16, 7, None, ALU.bitwise_and)
                v.tensor_scalar(b_s2, b_src_i16, 8, 2, ALU.bitwise_and, ALU.logical_shift_right)
            else:
                v.tensor_scalar(b_s1, b_src_i16, 4, 7, ALU.logical_shift_right, ALU.bitwise_and)
                v.tensor_scalar(b_s2, b_src_i16, 6, 2, ALU.logical_shift_right, ALU.bitwise_and)
            v.tensor_scalar(b_s3, b_s1, 2, -4, ALU.mult, ALU.add)
            v.tensor_tensor(b_s4, b_s1, b_s3, ALU.max)
            v.tensor_scalar(b_s3, b_s4, 2, -8, ALU.mult, ALU.add)
            v.tensor_tensor(b_s1, b_s4, b_s3, ALU.max)
            v.tensor_scalar(b_s2, b_s2, -1, 1, ALU.mult, ALU.add)
            v.tensor_tensor(b_s4, b_s1, b_s2, ALU.mult)

        def ffn(ap_src, ap_dst, dump_xt=False, skip_c=False):
            with ExitStack() as sub:
                sb_xt = [sbuf(sub, f"xt{c}", [128, S], BF16) for c in range(16)]
                ln_transpose(ap_src, sb_xt)
                if dump_xt:
                    with block("xtd") as blk:
                        @blk.gpsimd
                        def _(g_):
                            for c in range(16):
                                cOUT.dma(g_.dma_start(out=ap_xtd[128 * c:128 * (c + 1), :], in_=sb_xt[c][:, :]))
                            g_.wait_ge(cOUT.sem, cOUT.v)

                with ExitStack() as ph, block("ffnb") as blk:
                    b_pk = [sbuf(ph, f"fpk{m}", [128, 4096], U8) for m in range(2)]
                    b_sxs = [sbuf(ph, f"fsxs{i}", [128, 1024], F32) for i in range(2)]
                    b_sxu = sbuf(ph, "fsxu", [128, 4096], BF16)
                    b_sxg = sbuf(ph, "fsxg", [128, 4096], BF16)
                    b_b0 = sbuf(ph, "fb0", [128, 1024], I16)
                    b_s1 = sbuf(ph, "fs1", [128, 1024], I16)
                    b_s2 = sbuf(ph, "fs2", [128, 1024], I16)
                    b_s3 = sbuf(ph, "fs3", [128, 1024], I16)
                    b_s4 = sbuf(ph, "fs4", [128, 1024], I16)
                    b_wg = [[sbuf(ph, f"fwg{pp}{p}", [128, 4096], BF16) for p in range(2)] for pp in range(2)]
                    b_wu = [[sbuf(ph, f"fwu{pp}{p}", [128, 4096], BF16) for p in range(2)] for pp in range(2)]
                    b_gs = [sbuf(ph, f"fgs{i}", [128, 512], F32) for i in range(2)]
                    b_g2 = [sbuf(ph, f"fg2{i}", [128, 512], F32) for i in range(2)]
                    b_u1 = sbuf(ph, "fu1", [128, 512], F32)
                    b_v1 = sbuf(ph, "fv1", [128, 512], F32)
                    b_is = [sbuf(ph, f"fis{i}", [128, S], BF16) for i in range(2)]
                    p_g = [psum(ph, f"fpg{i}", [128, 512], F32) for i in range(3)]
                    p_u = [psum(ph, f"fpu{i}", [128, 512], F32) for i in range(3)]

                    ac0, ve0, pe0 = cAC.v, cVE.v, cPE.v
                    pkg0, pku0 = cPKG.v, cPKU.v
                    sxf0 = [cSXF[0].v, cSXF[1].v]
                    isb = [cIS[0].v, cIS[1].v]
                    NS = 32

                    ac_ms, ve_ms, pe_ms = {}, {}, {}
                    a = ac0
                    for qt_ in range(4):
                        a += 1; ac_ms[("cxu", 0, qt_)] = a
                    for qt_ in range(4):
                        a += 1; ac_ms[("cxg", 0, qt_)] = a
                    for s in range(NS):
                        for i in range(16):
                            if i % 2 == 0 and s + 1 < NS:
                                a += 1
                                ac_ms[("cxu", s + 1, i // 2) if i < 8 else ("cxg", s + 1, (i - 8) // 2)] = a
                            a += 1; ac_ms[("sq", s, i)] = a
                            a += 1; ac_ms[("v1a", s, i)] = a
                            a += 1; ac_ms[("th", s, i)] = a
                            a += 1; ac_ms[("v1b", s, i)] = a
                    ac_end = a
                    v_ = ve0
                    for mat in range(2):
                        for j in range(8):
                            if j == 6:
                                v_ += 1; ve_ms[("cu8" if mat == 0 else "cg8", 0)] = v_
                            v_ += 1; ve_ms[("uch" if mat == 0 else "gch", 0, j)] = v_
                    for s in range(NS):
                        for i in range(16):
                            v_ += 1; ve_ms[("u", s, i)] = v_
                            v_ += 1; ve_ms[("it", s, i)] = v_
                            if s + 1 < NS:
                                j = i if i < 8 else i - 8
                                if j == 6:
                                    v_ += 1; ve_ms[("cu8" if i < 8 else "cg8", s + 1)] = v_
                                v_ += 1; ve_ms[("uch" if i < 8 else "gch", s + 1, j)] = v_
                    ve_end = v_
                    p_ = pe0
                    for s in range(NS):
                        for i in range(16):
                            p_ += 1; pe_ms[("g", s, i)] = p_
                            p_ += 1; pe_ms[("up", s, i)] = p_
                    pe_end = p_

                    def slot_key(q):
                        s_, r_ = q // 8, q % 8
                        return ("cxu", s_, r_) if r_ < 4 else ("cxg", s_, r_ - 4)

                    @blk.sync
                    def _(e):
                        wait_outs(e)
                        for m in range(NS):
                            if m >= 1:
                                e.wait_ge(cVE.sem, ve_ms[("cu8", m - 1)])
                            cPKU.dma(e.dma_start(out=b_pk[1][:, :], in_=ap_pku[128 * m:128 * (m + 1), :]))
                            for r in range(4):
                                q = 8 * m + r
                                if q >= 2:
                                    e.wait_ge(cAC.sem, ac_ms[slot_key(q - 2)])
                                cSXF[q % 2].dma(e.dma_start(
                                    out=b_sxs[q % 2][:, :],
                                    in_=ap_scu[128 * m:128 * (m + 1), 1024 * r:1024 * (r + 1)]))
                            if m >= 1:
                                e.wait_ge(cVE.sem, ve_ms[("cg8", m - 1)])
                            cPKG.dma(e.dma_start(out=b_pk[0][:, :], in_=ap_pkg[128 * m:128 * (m + 1), :]))
                            for r in range(4):
                                q = 8 * m + 4 + r
                                if q >= 2:
                                    e.wait_ge(cAC.sem, ac_ms[slot_key(q - 2)])
                                cSXF[q % 2].dma(e.dma_start(
                                    out=b_sxs[q % 2][:, :],
                                    in_=ap_scg[128 * m:128 * (m + 1), 1024 * r:1024 * (r + 1)]))

                    @blk.scalar
                    def _(sc):
                        def castx(s, mat, qt_):
                            q = 8 * s + 4 * mat + qt_
                            sc.wait_ge(cSXF[q % 2].sem, sxf0[q % 2] + 16 * (q // 2 + 1))
                            if s >= 1:
                                sc.wait_ge(cVE.sem, ve_ms[("uch" if mat == 0 else "gch", s - 1, 2 * qt_ + 1)])
                            dst = b_sxu if mat == 0 else b_sxg
                            cAC.inc(sc.activation(dst[:, 1024 * qt_:1024 * (qt_ + 1)], b_sxs[q % 2][:, :],
                                                  AF.Identity, scale=0.5))

                        for qt_ in range(4):
                            castx(0, 0, qt_)
                        for qt_ in range(4):
                            castx(0, 1, qt_)
                        for s in range(NS):
                            for i in range(16):
                                if i % 2 == 0 and s + 1 < NS:
                                    castx(s + 1, 0 if i < 8 else 1, (i % 8) // 2)
                                sc.wait_ge(cPE.sem, pe_ms[("g", s, i)])
                                if i >= 2:
                                    sc.wait_ge(cVE.sem, ve_ms[("it", s, i - 2)])
                                elif s >= 1:
                                    sc.wait_ge(cVE.sem, ve_ms[("it", s - 1, 14 + i)])
                                sc.activation(b_gs[i % 2][:, :], p_g[(16 * s + i) % 3][:, :], AF.Copy)
                                cAC.inc(sc.activation(b_g2[i % 2][:, :], p_g[(16 * s + i) % 3][:, :], AF.Square))
                                if i >= 1:
                                    sc.wait_ge(cVE.sem, ve_ms[("it", s, i - 1)])
                                elif s >= 1:
                                    sc.wait_ge(cVE.sem, ve_ms[("it", s - 1, 15)])
                                cAC.inc(sc.activation(b_v1[:, :], b_g2[i % 2][:, :], AF.Identity,
                                                      scale=GC3, bias=sb_cst[:, 2:3]))
                                sc.wait_ge(cVE.sem, ve_ms[("u", s, i)])
                                cAC.inc(sc.activation(b_g2[i % 2][:, :], b_u1[:, :], AF.Tanh))
                                cAC.inc(sc.activation(b_v1[:, :], b_g2[i % 2][:, :], AF.Identity,
                                                      scale=0.5, bias=sb_cst[:, 3:4]))

                    @blk.tensor
                    def _(t):
                        for s in range(NS):
                            for i in range(16):
                                nt, tsl = i // 4, i % 4
                                if i == 0:
                                    t.wait_ge(cVE.sem, ve_ms[("gch", s, 7)])
                                if i >= 3:
                                    t.wait_ge(cAC.sem, ac_ms[("sq", s, i - 3)])
                                    t.wait_ge(cVE.sem, ve_ms[("it", s, i - 3)])
                                elif s >= 1:
                                    t.wait_ge(cAC.sem, ac_ms[("sq", s - 1, 13 + i)])
                                    t.wait_ge(cVE.sem, ve_ms[("it", s - 1, 13 + i)])
                                for c in range(16):
                                    w = b_wg[s % 2][0 if c < 8 else 1]
                                    cc = c % 8
                                    inst = t.matmul(p_g[(16 * s + i) % 3][:, :],
                                                    w[:, 512 * cc + 128 * nt:512 * cc + 128 * (nt + 1)],
                                                    sb_xt[c][:, 512 * tsl:512 * (tsl + 1)],
                                                    start=(c == 0), stop=(c == 15))
                                    if c == 15:
                                        cPE.inc(inst)
                                if i == 0:
                                    t.wait_ge(cVE.sem, ve_ms[("uch", s, 7)])
                                for c in range(16):
                                    w = b_wu[s % 2][0 if c < 8 else 1]
                                    cc = c % 8
                                    inst = t.matmul(p_u[(16 * s + i) % 3][:, :],
                                                    w[:, 512 * cc + 128 * nt:512 * cc + 128 * (nt + 1)],
                                                    sb_xt[c][:, 512 * tsl:512 * (tsl + 1)],
                                                    start=(c == 0), stop=(c == 15))
                                    if c == 15:
                                        cPE.inc(inst)

                    @blk.vector
                    def _(v):
                        def chunk(v, s, mat, j):
                            qt_, plane = j // 2, j % 2
                            sl = slice(1024 * qt_, 1024 * (qt_ + 1))
                            if j == 0:
                                if mat == 0:
                                    v.wait_ge(cPKU.sem, pku0 + 16 * (s + 1))
                                    if s >= 2:
                                        v.wait_ge(cPE.sem, pe_ms[("up", s - 2, 15)])
                                else:
                                    v.wait_ge(cPKG.sem, pkg0 + 16 * (s + 1))
                                    if s >= 2:
                                        v.wait_ge(cPE.sem, pe_ms[("g", s - 2, 15)])
                            if plane == 0:
                                inst = v.tensor_copy(b_b0[:, :], b_pk[1 - mat][:, sl])
                                if j == 6:
                                    cVE.inc(inst)
                            emit_decode(v, b_b0[:, :], b_s1[:, :], b_s2[:, :],
                                        b_s3[:, :], b_s4[:, :], plane, None)
                            v.wait_ge(cAC.sem, ac_ms[("cxu" if mat == 0 else "cxg", s, j // 2)])
                            wdst = (b_wu if mat == 0 else b_wg)[s % 2][plane]
                            bsx = b_sxu if mat == 0 else b_sxg
                            cVE.inc(v.tensor_tensor(wdst[:, sl], b_s4[:, :], bsx[:, sl], ALU.mult))

                        for mat in range(2):
                            for j in range(8):
                                chunk(v, 0, mat, j)
                        for s in range(NS):
                            for i in range(16):
                                nt, tsl = i // 4, i % 4
                                gnt = 4 * s + nt
                                v.wait_ge(cAC.sem, ac_ms[("v1a", s, i)])
                                cVE.inc(v.tensor_tensor(b_u1[:, :], b_v1[:, :], b_gs[i % 2][:, :], ALU.mult))
                                v.wait_ge(cAC.sem, ac_ms[("v1b", s, i)])
                                v.tensor_tensor(b_v1[:, :], b_v1[:, :], b_gs[i % 2][:, :], ALU.mult)
                                v.wait_ge(cPE.sem, pe_ms[("up", s, i)])
                                if tsl == 0 and gnt >= 2:
                                    v.wait_ge(cIS[gnt % 2].sem, isb[gnt % 2] + 16 * (gnt // 2))
                                cVE.inc(v.tensor_tensor(b_is[gnt % 2][:, 512 * tsl:512 * (tsl + 1)],
                                                        b_v1[:, :], p_u[(16 * s + i) % 3][:, :], ALU.mult))
                                if s + 1 < NS:
                                    chunk(v, s + 1, 0 if i < 8 else 1, i if i < 8 else i - 8)

                    @blk.gpsimd
                    def _(g):
                        for s in range(NS):
                            for nt in range(4):
                                gnt = 4 * s + nt
                                g.wait_ge(cVE.sem, ve_ms[("it", s, nt * 4 + 3)])
                                cIS[gnt % 2].dma(g.dma_start(out=ap_intr[128 * gnt:128 * (gnt + 1), :], in_=b_is[gnt % 2][:, :]))

                    assert cPKG.v == pkg0 + 512 and cPKU.v == pku0 + 512
                    assert cSXF[0].v == sxf0[0] + 16 * 128 and cSXF[1].v == sxf0[1] + 16 * 128
                    assert cAC.v == ac_end, (cAC.v, ac_end)
                    assert cVE.v == ve_end, (cVE.v, ve_end)
                    assert cPE.v == pe_end, (cPE.v, pe_end)
                    assert cIS[0].v == isb[0] + 64 * 16 and cIS[1].v == isb[1] + 64 * 16

            if skip_c:
                return
            with ExitStack() as ph, block("ffnc") as blk:
                b_acc = [sbuf(ph, f"dac{i}", [128, 1024], F32) for i in range(16)]
                b_it = [sbuf(ph, f"dit{i}", [128, S], BF16) for i in range(16)]
                b_pk = sbuf(ph, "dpk", [128, 4096], U8)
                sxs = [sbuf(ph, f"dsxs{i}", [128, 1024], F32) for i in range(2)]
                b_sx = sbuf(ph, "dsx", [128, 4096], BF16)
                b_b16 = sbuf(ph, "db16", [128, 4096], I16)
                b_s1 = sbuf(ph, "ds1", [128, 2048], I16)
                b_s2 = sbuf(ph, "ds2", [128, 2048], I16)
                b_s3 = sbuf(ph, "ds3", [128, 2048], I16)
                b_s4 = sbuf(ph, "ds4", [128, 2048], I16)
                b_wd = [[sbuf(ph, f"dwd{pp}{p}", [128, 4096], BF16) for p in range(2)] for pp in range(2)]
                p_d = [psum(ph, f"dpd{i}", [128, 1024], F32) for i in range(3)]

                ac0, ve0, pe0 = cAC.v, cVE.v, cPE.v
                it0 = [cIT[0].v, cIT[1].v]
                pkd0, acc0, fl0 = cPKD.v, cACC.v, cFL.v
                sc20 = [cSC2[0].v, cSC2[1].v]
                NG = 32

                ac_ms, ve_ms, pe_ms = {}, {}, {}
                a = ac0
                for c in range(4):
                    a += 1; ac_ms[("cxd", 0, c)] = a
                for m in range(NG):
                    if m + 1 < NG:
                        for c in range(4):
                            a += 1; ac_ms[("cxd", m + 1, c)] = a
                ac_end = a
                v_ = ve0
                v_ += 1; ve_ms[("c16", 0)] = v_
                for c in range(4):
                    v_ += 1; ve_ms[("dw", 0, c)] = v_
                for m in range(NG):
                    for tt in range(16):
                        v_ += 1; ve_ms[("ac", m, tt)] = v_
                        if tt % 4 == 1 and m + 1 < NG:
                            c = tt // 4
                            if c == 0:
                                v_ += 1; ve_ms[("c16", m + 1)] = v_
                            v_ += 1; ve_ms[("dw", m + 1, c)] = v_
                ve_end = v_
                p_ = pe0
                for m in range(NG):
                    for tt in range(16):
                        p_ += 1; pe_ms[("mm", m, tt)] = p_
                pe_end = p_

                @blk.sync
                def _(e):
                    wait_outs(e)
                    for m in range(NG):
                        h2, g = m // 16, m % 16
                        if m >= 2:
                            e.wait_ge(cPE.sem, pe_ms[("mm", m - 2, 15)])
                        for k in range(8):
                            j = 4 * g + k if k < 4 else 64 + 4 * g + (k - 4)
                            slot = (m % 2) * 8 + k
                            cIT[m % 2].dma(e.dma_start(out=b_it[slot][:, :], in_=ap_intr[128 * j:128 * (j + 1), :]))
                        if m >= 1:
                            e.wait_ge(cVE.sem, ve_ms[("c16", m - 1)])
                        cPKD.dma(e.dma_start(out=b_pk[:, :],
                                             in_=ap_pkd[256 * g + 128 * h2:256 * g + 128 * (h2 + 1), :]))
                        for c in range(4):
                            q = 4 * m + c
                            if q >= 2:
                                q2 = q - 2
                                e.wait_ge(cAC.sem, ac_ms[("cxd", q2 // 4, q2 % 4)])
                            cSC2[q % 2].dma(e.dma_start(
                                out=sxs[q % 2][:, :],
                                in_=ap_scd[128 * m:128 * (m + 1), 1024 * c:1024 * (c + 1)]))
                        if g == 0:
                            if h2 == 1:
                                e.wait_ge(cFL.sem, fl0 + 16 * 16)
                            for i in range(16):
                                cACC.dma(e.dma_start(out=b_acc[i][:, :],
                                                     in_=ap_src[128 * i:128 * (i + 1), 1024 * h2:1024 * (h2 + 1)]))

                @blk.scalar
                def _(sc):
                    def chain(m):
                        for c in range(4):
                            q = 4 * m + c
                            sc.wait_ge(cSC2[q % 2].sem, sc20[q % 2] + 16 * (q // 2 + 1))
                            if m >= 1:
                                sc.wait_ge(cVE.sem, ve_ms[("dw", m - 1, 2 * (c // 2) + 1)])
                            cAC.inc(sc.activation(b_sx[:, 1024 * c:1024 * (c + 1)], sxs[q % 2][:, :],
                                                  AF.Identity, scale=0.25))

                    chain(0)
                    for m in range(NG):
                        if m + 1 < NG:
                            chain(m + 1)

                @blk.tensor
                def _(t):
                    for m in range(NG):
                        for tt in range(16):
                            if tt == 0:
                                t.wait_ge(cIT[m % 2].sem, it0[m % 2] + 128 * (m // 2 + 1))
                                t.wait_ge(cVE.sem, ve_ms[("dw", m, 3)])
                            if tt >= 3:
                                t.wait_ge(cVE.sem, ve_ms[("ac", m, tt - 3)])
                            elif m >= 1:
                                t.wait_ge(cVE.sem, ve_ms[("ac", m - 1, 13 + tt)])
                            for k in range(8):
                                c4, plane = k % 4, k // 4
                                slot = (m % 2) * 8 + (c4 if plane == 0 else 4 + c4)
                                wsrc = b_wd[m % 2][plane]
                                for j in range(2):
                                    inst = t.matmul(p_d[(16 * m + tt) % 3][:, 512 * j:512 * (j + 1)],
                                                    b_it[slot][:, 128 * tt:128 * (tt + 1)],
                                                    wsrc[:, 1024 * c4 + 512 * j:1024 * c4 + 512 * (j + 1)],
                                                    start=(k == 0), stop=(k == 7))
                                    if k == 7 and j == 1:
                                        cPE.inc(inst)

                @blk.vector
                def _(v):
                    def chunk(v, m, c):
                        half, plane = c // 2, c % 2
                        sl = slice(2048 * half, 2048 * (half + 1))
                        if c == 0:
                            v.wait_ge(cPKD.sem, pkd0 + 16 * (m + 1))
                            if m >= 2:
                                v.wait_ge(cPE.sem, pe_ms[("mm", m - 2, 15)])
                            cVE.inc(v.tensor_copy(b_b16[:, :], b_pk[:, :]))
                        emit_decode(v, b_b16[:, sl], b_s1[:, :], b_s2[:, :], b_s3[:, :], b_s4[:, :], plane, None)
                        v.wait_ge(cAC.sem, ac_ms[("cxd", m, 2 * half + 1)])
                        cVE.inc(v.tensor_tensor(b_wd[m % 2][plane][:, sl], b_s4[:, :], b_sx[:, sl], ALU.mult))

                    for c in range(4):
                        chunk(v, 0, c)
                    for m in range(NG):
                        h2, g = m // 16, m % 16
                        for tt in range(16):
                            v.wait_ge(cPE.sem, pe_ms[("mm", m, tt)])
                            if tt == 0 and g == 0:
                                v.wait_ge(cACC.sem, acc0 + 256 * (h2 + 1))
                            cVE.inc(v.tensor_tensor(b_acc[tt][:, :], b_acc[tt][:, :],
                                                    p_d[(16 * m + tt) % 3][:, :], ALU.add))
                            if tt % 4 == 1 and m + 1 < NG:
                                chunk(v, m + 1, tt // 4)

                @blk.gpsimd
                def _(g_):
                    for h2 in range(2):
                        for i in range(16):
                            g_.wait_ge(cVE.sem, ve_ms[("ac", h2 * 16 + 15, i)])
                            cFL.dma(g_.dma_start(out=ap_dst[128 * i:128 * (i + 1), 1024 * h2:1024 * (h2 + 1)],
                                                 in_=b_acc[i][:, :]))

                assert cIT[0].v == it0[0] + 128 * 16 and cIT[1].v == it0[1] + 128 * 16
                assert cPKD.v == pkd0 + 512
                assert cSC2[0].v == sc20[0] + 16 * 64 and cSC2[1].v == sc20[1] + 16 * 64
                assert cACC.v == acc0 + 512
                assert cAC.v == ac_end, (cAC.v, ac_end)
                assert cVE.v == ve_end, (cVE.v, ve_end)
                assert cPE.v == pe_end, (cPE.v, pe_end)
                assert cFL.v == fl0 + 512

        def attn(ap_src, ap_dst):
            with ExitStack() as sub:
                with ExitStack() as ph:
                    sb_xt = [sbuf(ph, f"axt{c}", [128, S], BF16) for c in range(16)]
                    ln_transpose(ap_src, sb_xt)

                    with ExitStack() as ph2, block("proj") as blk:
                        b_wf = [sbuf(ph2, f"awf{i}", [128, D], F32) for i in range(2)]
                        b_wb = [sbuf(ph2, f"awb{c}", [128, D], BF16) for c in range(16)]
                        b_st = [sbuf(ph2, f"ast{i}", [128, 512], BF16) for i in range(2)]
                        sb_bvb = sbuf(ph2, "sb_bvb", [128, D], F32)
                        p_p = [psum(ph2, f"app{i}", [128, 512], F32) for i in range(2)]

                        ac0, ve0, pe0 = cAC.v, cVE.v, cPE.v
                        wf0 = [cWF[0].v, cWF[1].v]
                        one0 = cONE.v
                        w_src = [ap_wq, ap_wk, ap_wv]
                        o_dst = [ap_qt, ap_kt, None]
                        st_hist = [cST[0].v, cST[1].v]
                        st_base = [cST[0].v, cST[1].v]
                        st_wait = {}
                        for m_ in range(3):
                            for g_ in range(64):
                                st_wait[(m_, g_)] = st_hist[g_ % 2]
                                st_hist[g_ % 2] += 16 * (1 if m_ < 2 else 4)

                        def _castv(m, c):
                            return ve0 + 16 * m + c + 1

                        def _vevac(g):
                            return ve0 + 48 + g + 1

                        @blk.sync
                        def _(e):
                            wait_outs(e)
                            cONE.dma(e.dma_start(out=sb_bvb[:, :], in_=ap_bvb))
                            for m in range(3):
                                for c in range(16):
                                    k = m * 16 + c
                                    if k >= 2:
                                        e.wait_ge(cVE.sem, _castv((k - 2) // 16, (k - 2) % 16))
                                    cWF[k % 2].dma(e.dma_start(out=b_wf[c % 2][:, :],
                                                               in_=w_src[m][128 * c:128 * (c + 1), :]))

                        @blk.vector
                        def _(v):
                            v.wait_ge(cONE.sem, one0 + 16)
                            for m in range(3):
                                for c in range(16):
                                    k = m * 16 + c
                                    v.wait_ge(cWF[k % 2].sem, wf0[k % 2] + 16 * (k // 2 + 1))
                                    if m > 0 and c == 0:
                                        v.wait_ge(cPE.sem, pe0 + m * 64)
                                    cVE.inc(v.tensor_copy(b_wb[c][:, :], b_wf[c % 2][:, :]))
                            for g in range(64):
                                tt_i, j = g // 4, g % 4
                                v.wait_ge(cPE.sem, pe0 + 2 * 64 + g + 1)
                                if st_wait[(2, g)] > st_base[g % 2]:
                                    v.wait_ge(cST[g % 2].sem, st_wait[(2, g)])
                                cVE.inc(v.scalar_tensor_tensor(
                                    out=b_st[g % 2][:, :],
                                    in0=sb_bvb[:, 512 * j:512 * (j + 1)], scalar=1.0,
                                    in1=p_p[g % 2][:, :], op0=ALU.mult, op1=ALU.add))

                        @blk.tensor
                        def _(t):
                            for m in range(3):
                                t.wait_ge(cVE.sem, _castv(m, 15))
                                for g in range(64):
                                    ot, j = g // 4, g % 4
                                    G = m * 64 + g
                                    if G >= 2:
                                        pm, pg2 = divmod(G - 2, 64)
                                        if pm < 2:
                                            t.wait_ge(cAC.sem, ac0 + pm * 64 + pg2 + 1)
                                        else:
                                            t.wait_ge(cVE.sem, _vevac(pg2))
                                    for c in range(16):
                                        if m < 2:
                                            inst = t.matmul(p_p[g % 2][:, :],
                                                            b_wb[c][:, 128 * ot:128 * (ot + 1)],
                                                            sb_xt[c][:, 512 * j:512 * (j + 1)],
                                                            start=(c == 0), stop=(c == 15))
                                        else:
                                            inst = t.matmul(p_p[g % 2][:, :],
                                                            sb_xt[c][:, 128 * ot:128 * (ot + 1)],
                                                            b_wb[c][:, 512 * j:512 * (j + 1)],
                                                            start=(c == 0), stop=(c == 15))
                                        if c == 15:
                                            cPE.inc(inst)

                        @blk.scalar
                        def _(sc):
                            for m in range(2):
                                bias = sb_bqs if m == 0 else sb_bk
                                for g in range(64):
                                    ot = g // 4
                                    sc.wait_ge(cPE.sem, pe0 + m * 64 + g + 1)
                                    if st_wait[(m, g)] > st_base[g % 2]:
                                        sc.wait_ge(cST[g % 2].sem, st_wait[(m, g)])
                                    if m == 0:
                                        cAC.inc(sc.activation(b_st[g % 2][:, :], p_p[g % 2][:, :], AF.Identity,
                                                              bias=bias[:, ot:ot + 1], scale=sb_cst[:, 1:2]))
                                    else:
                                        cAC.inc(sc.activation(b_st[g % 2][:, :], p_p[g % 2][:, :], AF.Identity,
                                                              bias=bias[:, ot:ot + 1]))

                        @blk.gpsimd
                        def _(g_):
                            for m in range(2):
                                dst = o_dst[m]
                                for g in range(64):
                                    ot, j = g // 4, g % 4
                                    g_.wait_ge(cAC.sem, ac0 + m * 64 + g + 1)
                                    cST[g % 2].dma(g_.dma_start(out=dst[128 * ot:128 * (ot + 1), 512 * j:512 * (j + 1)],
                                                                in_=b_st[g % 2][:, :]))
                            for g in range(64):
                                ot, j = g // 4, g % 4
                                g_.wait_ge(cVE.sem, _vevac(g))
                                for hh in range(4):
                                    cST[g % 2].dma(g_.dma_start(
                                        out=ap_vt[(4 * j + hh) * 128:(4 * j + hh + 1) * 128, 128 * ot:128 * (ot + 1)],
                                        in_=b_st[g % 2][:, 128 * hh:128 * (hh + 1)]))

                        assert cWF[0].v == wf0[0] + 24 * 16 and cWF[1].v == wf0[1] + 24 * 16
                        assert cVE.v == ve0 + 48 + 64
                        assert cPE.v == pe0 + 3 * 64
                        assert cAC.v == ac0 + 2 * 64
                        assert cST[0].v == st_hist[0] and cST[1].v == st_hist[1]

                sb_ctx = [sbuf(sub, f"actx{h}", [128, S], BF16) for h in range(NH)]
                with ExitStack() as ph, block("heads") as blk:
                    b_kh = [sbuf(ph, f"akh{i}", [128, S], BF16) for i in range(2)]
                    b_qh = [sbuf(ph, f"aqh{i}", [128, S], BF16) for i in range(2)]
                    b_vh = [sbuf(ph, f"avh{i}", [128, S], BF16) for i in range(2)]
                    b_p3 = [sbuf(ph, f"ap3{i}", [128, S], BF16) for i in range(2)]
                    sb_al = sbuf(ph, "sb_al", [128, S], F32)
                    b_af = sbuf(ph, "aaf", [128, S], F32)
                    b_amdt = sbuf(ph, "aamdt", [128, S], F32)
                    b_ptt = sbuf(ph, "aptt", [128, 16 * 512], BF16)
                    b_sst = sbuf(ph, "asst", [128, 6], F32)
                    sb_tmt = sbuf(ph, "sb_tmt", [128, S], F32)
                    sb_nq = sbuf(ph, "sb_nq", [128, 256], F32)
                    p_s = psum(ph, "aps", [128, 2048], F32)
                    p_tw = [psum(ph, f"aptw{i}", [128, 512], BF16) for i in range(2)]
                    p_av = psum(ph, "apav", [128, 512], F32)

                    ac0, ve0, pe0 = cAC.v, cVE.v, cPE.v
                    kq0 = [cKQ[0].v, cKQ[1].v]
                    one0 = cONE.v
                    pt3 = b_ptt[:, :].rearrange("p (kc u) -> p kc u", u=512)

                    ITERS = [(h, qi) for h in range(NH) for qi in range(16)]
                    prev_of = {ITERS[i]: ITERS[i - 1] for i in range(1, len(ITERS))}
                    prev_of[ITERS[0]] = None
                    UNITS = [(it, prev_of[it]) for it in ITERS] + [(None, ITERS[-1])]

                    pe_sched, ve_sched, ac_sched = {}, {}, {}
                    pe_c, ve_c, ac_c = pe0, ve0, ac0
                    for it, pq in UNITS:
                        if it is not None:
                            nb_ = (it[1] + 4) // 4
                            for b in range(nb_):
                                pe_c += 1
                                pe_sched[("scb",) + it + (b,)] = pe_c
                            pe_sched[("score",) + it] = pe_c
                        if pq is not None:
                            hp, qp = pq
                            for j in range((qp + 4) // 4):
                                pe_c += 1
                                pe_sched[("trg",) + pq + (j,)] = pe_c
                            if qp % 4 == 3:
                                pe_c += 1
                                pe_sched[("av", hp, qp // 4)] = pe_c
                        if it is not None:
                            nb_ = (it[1] + 4) // 4
                            for b in range(nb_ - 1):
                                ve_c += 1
                                ve_sched[("addb",) + it + (b,)] = ve_c
                            ve_c += 1
                            ve_sched[("diag",) + it] = ve_c
                        if pq is not None:
                            hp, qp = pq
                            for j in range((qp + 4) // 4):
                                if _TRG_IDX[pq + (j,)] % 2 == 0:
                                    ve_c += 1
                                    ve_sched[("tge",) + pq + (j,)] = ve_c
                            if qp % 4 == 0:
                                ve_c += 1
                                ve_sched[("ms", hp, qp // 4)] = ve_c
                        if it is not None:
                            ve_c += 1
                            ve_sched[("soft",) + it] = ve_c
                            ac_c += 1
                            ac_sched[("exp",) + it] = ac_c
                        if pq is not None:
                            hp, qp = pq
                            for j in range((qp + 4) // 4):
                                if _TRG_IDX[pq + (j,)] % 2 == 1:
                                    ac_c += 1
                                    ac_sched[("tge",) + pq + (j,)] = ac_c
                            if qp % 4 == 3:
                                ac_c += 1
                                ac_sched[("avev", hp, qp // 4)] = ac_c

                    def grp_eng(hp, qp, j):
                        return _TRG_IDX[(hp, qp, j)] % 2

                    def last_grp(hp, qp, eng):
                        js = [j for j in range((qp + 4) // 4) if grp_eng(hp, qp, j) == eng]
                        return js[-1] if js else None

                    @blk.sync
                    def _(e):
                        wait_outs(e)
                        cONE.dma(e.dma_start(out=sb_tmt[:, :], in_=ap_tmt))
                        cONE.dma(e.dma_start(out=sb_nq[:, :], in_=ap_negafq))
                        cONE.dma(e.dma_start(out=sb_al[:, :], in_=ap_alibi))
                        for h in range(NH):
                            if h >= 2:
                                e.wait_ge(cPE.sem, pe_sched[("av", h - 2, 3)])
                            cKQ[h % 2].dma(e.dma_start(out=b_kh[h % 2][:, :], in_=ap_kt[128 * h:128 * (h + 1), :]))
                            cKQ[h % 2].dma(e.dma_start(out=b_qh[h % 2][:, :], in_=ap_qt[128 * h:128 * (h + 1), :]))
                            cKQ[h % 2].dma(e.dma_start(out=b_vh[h % 2][:, :], in_=ap_vt[128 * h:128 * (h + 1), :]))

                    @blk.tensor
                    def _(t):
                        for it, pq in UNITS:
                            if it is not None:
                                h, qi = it
                                if qi == 0:
                                    t.wait_ge(cKQ[h % 2].sem, kq0[h % 2] + 48 * (h // 2 + 1))
                                klen = 128 * (qi + 1)
                                if pq is not None:
                                    t.wait_ge(cAC.sem, ac_sched[("exp",) + pq])
                                nb = (klen + 511) // 512
                                for b in range(nb):
                                    w = min(512, klen - 512 * b)
                                    cPE.inc(t.matmul(p_s[:, 512 * b:512 * b + w],
                                                     b_qh[h % 2][:, 128 * qi:128 * (qi + 1)],
                                                     b_kh[h % 2][:, 512 * b:512 * b + w],
                                                     start=True, stop=True))
                            if pq is None:
                                continue
                            hp, qp = pq
                            t.wait_ge(cVE.sem, ve_sched[("soft",) + pq])
                            for kc in range(qp + 1):
                                j = kc // 4
                                gi = _TRG_IDX[pq + (j,)]
                                if kc % 4 == 0 and gi >= 2:
                                    pk = _TRG_SEQ[gi - 2]
                                    sched = ve_sched if _TRG_IDX[pk] % 2 == 0 else ac_sched
                                    sem = cVE.sem if _TRG_IDX[pk] % 2 == 0 else cAC.sem
                                    t.wait_ge(sem, sched[("tge",) + pk])
                                inst = t.transpose(p_tw[gi % 2][:, 128 * (kc % 4):128 * (kc % 4 + 1)],
                                                   b_p3[qp % 2][:, 128 * kc:128 * (kc + 1)], sb_id[:, :])
                                if kc % 4 == 3 or kc == qp:
                                    cPE.inc(inst)
                            if qp % 4 == 3:
                                g = qp // 4
                                for eng in (0, 1):
                                    jl = last_grp(hp, qp, eng)
                                    if jl is not None:
                                        sched = ve_sched if eng == 0 else ac_sched
                                        sem = cVE.sem if eng == 0 else cAC.sem
                                        t.wait_ge(sem, sched[("tge",) + pq + (jl,)])
                                if (hp, g) != (0, 0):
                                    pg = (hp, g - 1) if g > 0 else (hp - 1, 3)
                                    t.wait_ge(cAC.sem, ac_sched[("avev",) + pg])
                                for kc in range(4 * (g + 1)):
                                    inst = t.matmul(p_av[:, :], b_vh[hp % 2][:, 128 * kc:128 * (kc + 1)],
                                                    b_ptt[:, 512 * kc:512 * (kc + 1)],
                                                    start=(kc == 0), stop=(kc == 4 * g + 3))
                                    if kc == 4 * g + 3:
                                        cPE.inc(inst)

                    @blk.vector
                    def _(v):
                        for it, pq in UNITS:
                            if it is not None:
                                h, qi = it
                                if qi == 0:
                                    if h == 0:
                                        v.wait_ge(cONE.sem, one0 + 48)
                                    slope = SLOPES[h]
                                    v.tensor_scalar(b_af[:, :], sb_al[:, :], slope, None, ALU.mult)
                                    v.tensor_tensor(b_amdt[:, :], b_af[:, :], sb_tmt[:, :], ALU.add)
                                dg = slice(qi * 128, qi * 128 + 128)
                                b_d = (qi * 128) // 512
                                o = qi * 128 - 512 * b_d
                                for b in range(b_d):
                                    v.wait_ge(cPE.sem, pe_sched[("scb",) + it + (b,)])
                                    cVE.inc(v.tensor_tensor(p_s[:, 512 * b:512 * (b + 1)],
                                                            p_s[:, 512 * b:512 * (b + 1)],
                                                            b_af[:, 512 * b:512 * (b + 1)], ALU.add))
                                v.wait_ge(cPE.sem, pe_sched[("score",) + it])
                                if o > 0:
                                    v.tensor_tensor(p_s[:, 512 * b_d:512 * b_d + o],
                                                    p_s[:, 512 * b_d:512 * b_d + o],
                                                    b_af[:, 512 * b_d:512 * b_d + o], ALU.add)
                                cVE.inc(v.tensor_tensor(p_s[:, dg], p_s[:, dg], b_amdt[:, dg], ALU.add))
                            if pq is not None:
                                hp, qp = pq
                                for j in range((qp + 4) // 4):
                                    if grp_eng(hp, qp, j) == 0:
                                        gi = _TRG_IDX[pq + (j,)]
                                        v.wait_ge(cPE.sem, pe_sched[("trg",) + pq + (j,)])
                                        nk = min(qp, 4 * j + 3) - 4 * j + 1
                                        cVE.inc(v.tensor_copy(
                                            pt3[:, 4 * j:4 * j + nk, 128 * (qp % 4):128 * (qp % 4 + 1)],
                                            p_tw[gi % 2][:, 0:128 * nk].rearrange("p (t u) -> p t u", u=128)))
                                if qp % 4 == 0:
                                    g = qp // 4
                                    if (hp, g) != (0, 0):
                                        pg = (hp, g - 1) if g > 0 else (hp - 1, 3)
                                        v.wait_ge(cPE.sem, pe_sched[("av",) + pg])
                                    if g < 3:
                                        cVE.inc(v.memset(b_ptt[:, 512 * (4 * g + 1):512 * (4 * g + 4)], 0))
                                    else:
                                        cVE.inc(v.memset(b_ptt[:, 512 * 13:512 * 16], 0))
                            if it is not None:
                                h, qi = it
                                klen = 128 * (qi + 1)
                                nb = (klen + 511) // 512
                                v.wait_ge(cAC.sem, ac_sched[("exp",) + it])
                                if nb > 1:
                                    v.tensor_tensor(b_sst[:, 3:4], b_sst[:, 4:5], b_sst[:, 5:6], ALU.add)
                                    v.drain()
                                v.reciprocal(b_sst[:, 2:3], b_sst[:, 3:4])
                                v.drain()
                                cVE.inc(v.tensor_scalar(b_p3[qi % 2][:, 0:klen], b_p3[qi % 2][:, 0:klen],
                                                        b_sst[:, 2:3], None, ALU.mult))

                    @blk.scalar
                    def _(sc):
                        for it, pq in UNITS:
                            if it is not None:
                                h, qi = it
                                klen = 128 * (qi + 1)
                                nb = (klen + 511) // 512
                                bias = sb_nq[:, 16 * h + qi:16 * h + qi + 1]
                                if nb > 1:
                                    sc.wait_ge(cVE.sem, ve_sched[("addb",) + it + (nb - 2,)])
                                    sc.activation(b_p3[qi % 2][:, 0:512 * (nb - 1)], p_s[:, 0:512 * (nb - 1)],
                                                  AF.Exp, bias=bias, accum_out=b_sst[:, 4:5])
                                sc.wait_ge(cVE.sem, ve_sched[("diag",) + it])
                                cAC.inc(sc.activation(b_p3[qi % 2][:, 512 * (nb - 1):klen],
                                                      p_s[:, 512 * (nb - 1):klen], AF.Exp, bias=bias,
                                                      accum_out=b_sst[:, 5:6] if nb > 1 else b_sst[:, 3:4]))
                            if pq is None:
                                continue
                            hp, qp = pq
                            g4 = qp // 4
                            for j in range((qp + 4) // 4):
                                if grp_eng(hp, qp, j) == 1:
                                    gi = _TRG_IDX[pq + (j,)]
                                    sc.wait_ge(cPE.sem, pe_sched[("trg",) + pq + (j,)])
                                    if j == g4 and qp % 4 != 0:
                                        sc.wait_ge(cVE.sem, ve_sched[("ms", hp, g4)])
                                    nk = min(qp, 4 * j + 3) - 4 * j + 1
                                    cAC.inc(sc.activation(
                                        pt3[:, 4 * j:4 * j + nk, 128 * (qp % 4):128 * (qp % 4 + 1)],
                                        p_tw[gi % 2][:, 0:128 * nk].rearrange("p (t u) -> p t u", u=128),
                                        AF.Copy))
                            if qp % 4 == 3:
                                sc.wait_ge(cPE.sem, pe_sched[("av", hp, g4)])
                                cAC.inc(sc.activation(sb_ctx[hp][:, 512 * g4:512 * (g4 + 1)], p_av[:, :], AF.Copy))

                    assert cPE.v == pe_c, (cPE.v, pe_c)
                    assert cVE.v == ve_c, (cVE.v, ve_c)
                    assert cAC.v == ac_c, (cAC.v, ac_c)
                    assert cKQ[0].v == kq0[0] + 48 * 8 and cKQ[1].v == kq0[1] + 48 * 8
                with ExitStack() as ph, block("oproj") as blk:
                    b_wf = [sbuf(ph, f"owf{i}", [128, D], F32) for i in range(2)]
                    b_wb = [sbuf(ph, f"owb{c}", [128, D], BF16) for c in range(16)]
                    b_x1 = [sbuf(ph, f"ox1{i}", [128, D], F32) for i in range(2)]
                    b_o = [sbuf(ph, f"oo{i}", [128, D], F32) for i in range(2)]
                    sb_bob = sbuf(ph, "sb_bob", [128, D], F32)
                    p_p = [psum(ph, f"opp{i}", [128, 512], F32) for i in range(2)]

                    ac0, ve0, pe0 = cAC.v, cVE.v, cPE.v
                    wf0 = [cWF[0].v, cWF[1].v]
                    x0 = [cX[0].v, cX[1].v]
                    fo0 = [cFO[0].v, cFO[1].v]
                    one0 = cONE.v
                    _vc = lambda c: ve0 + c + 1
                    _vx = lambda i: ve0 + 16 + 5 * i + 1
                    _va = lambda i, j: ve0 + 16 + 5 * i + 1 + j + 1

                    @blk.sync
                    def _(e):
                        wait_outs(e)
                        cONE.dma(e.dma_start(out=sb_bob[:, :], in_=ap_bob))
                        for c in range(16):
                            if c >= 2:
                                e.wait_ge(cVE.sem, _vc(c - 2))
                            cWF[c % 2].dma(e.dma_start(out=b_wf[c % 2][:, :], in_=ap_wo[128 * c:128 * (c + 1), :]))
                        for i in range(16):
                            if i >= 2:
                                e.wait_ge(cVE.sem, _vx(i - 2))
                            cX[i % 2].dma(e.dma_start(out=b_x1[i % 2][:, :], in_=ap_src[128 * i:(i + 1) * 128, :]))

                    @blk.vector
                    def _(v):
                        v.wait_ge(cONE.sem, one0 + 16)
                        for c in range(16):
                            v.wait_ge(cWF[c % 2].sem, wf0[c % 2] + 16 * (c // 2 + 1))
                            cVE.inc(v.tensor_copy(b_wb[c][:, :], b_wf[c % 2][:, :]))
                        for i in range(16):
                            v.wait_ge(cX[i % 2].sem, x0[i % 2] + 16 * (i // 2 + 1))
                            if i >= 2:
                                v.wait_ge(cFO[i % 2].sem, fo0[i % 2] + 16 * (i // 2))
                            cVE.inc(v.tensor_tensor(b_o[i % 2][:, :], b_x1[i % 2][:, :], sb_bob[:, :], ALU.add))
                            for j in range(4):
                                v.wait_ge(cPE.sem, pe0 + 4 * i + j + 1)
                                cVE.inc(v.tensor_tensor(b_o[i % 2][:, 512 * j:512 * (j + 1)],
                                                        b_o[i % 2][:, 512 * j:512 * (j + 1)],
                                                        p_p[j % 2][:, :], ALU.add))

                    @blk.tensor
                    def _(t):
                        t.wait_ge(cVE.sem, _vc(15))
                        for i in range(16):
                            for j in range(4):
                                g = 4 * i + j
                                if g >= 2:
                                    t.wait_ge(cVE.sem, _va((g - 2) // 4, (g - 2) % 4))
                                for h in range(16):
                                    inst = t.matmul(p_p[j % 2][:, :], sb_ctx[h][:, 128 * i:128 * (i + 1)],
                                                    b_wb[h][:, 512 * j:512 * (j + 1)],
                                                    start=(h == 0), stop=(h == 15))
                                    if h == 15:
                                        cPE.inc(inst)

                    @blk.gpsimd
                    def _(g_):
                        for i in range(16):
                            g_.wait_ge(cVE.sem, _va(i, 3))
                            cFO[i % 2].dma(g_.dma_start(out=ap_dst[128 * i:128 * (i + 1), :], in_=b_o[i % 2][:, :]))

                    assert cWF[0].v == wf0[0] + 8 * 16 and cWF[1].v == wf0[1] + 8 * 16
                    assert cX[0].v == x0[0] + 8 * 16 and cX[1].v == x0[1] + 8 * 16
                    assert cVE.v == ve0 + 16 + 16 * 5
                    assert cPE.v == pe0 + 64
                    assert cFO[0].v == fo0[0] + 8 * 16 and cFO[1].v == fo0[1] + 8 * 16

        ffn(ap_x, ap_x1, dump_xt="xtd" in debug_out, skip_c=(stop_after == "ffnb"))
        if stop_after not in ("ffn1", "ffnb"):
            attn(ap_x1, ap_x2)
            ffn(ap_x2, ap_out)

        with nc.Block("fin") as blk:
            @blk.gpsimd
            def _(g_):
                for c_ in OUT_CTRS:
                    if c_.v:
                        g_.wait_ge(c_.sem, c_.v)

    return nc



_DPERM = np.concatenate([np.arange(0, D, 2), np.arange(1, D, 2)])
_NPERM = np.concatenate([np.arange(0, NM, 2), np.arange(1, NM, 2)])
_INV_DPERM = np.argsort(_DPERM)


def _bytes_of_packed(arr_i32, n_rows, n_half):
    a = np.ascontiguousarray(np.asarray(arr_i32, dtype=np.int32))
    return a.view(np.uint8).reshape(n_rows, n_half * 4)[:, ::4]


def host_prep(inputs):
    x = np.asarray(inputs["L0_attn_x"], np.float32)
    wq = np.ascontiguousarray(np.asarray(inputs["L0_attn_Wq"], np.float32)[_DPERM, :])
    wk = np.ascontiguousarray(np.asarray(inputs["L0_attn_Wk"], np.float32)[_DPERM, :])
    wv = np.ascontiguousarray(np.asarray(inputs["L0_attn_Wv"], np.float32)[_DPERM, :])
    wo = np.ascontiguousarray(np.asarray(inputs["L0_attn_Wo"], np.float32)[:, _DPERM])
    bq = np.ascontiguousarray(np.asarray(inputs["L0_attn_bq"], np.float32).reshape(16, 128).T)
    bk = np.ascontiguousarray(np.asarray(inputs["L0_attn_bk"], np.float32).reshape(16, 128).T)
    bvb = np.ascontiguousarray(np.broadcast_to(np.asarray(inputs["L0_attn_bv"], np.float32), (128, D)))
    bob = np.ascontiguousarray(np.broadcast_to(np.asarray(inputs["L0_attn_bo"], np.float32)[_DPERM], (128, D)))

    pkg = _bytes_of_packed(inputs["L0_ffn_gate_W_packed"], NM, 1024)
    pku = _bytes_of_packed(inputs["L0_ffn_up_W_packed"], NM, 1024)
    pkd = _bytes_of_packed(inputs["L0_ffn_down_W_packed"], D, 8192)
    pkgT = pkg.T[:, _NPERM]
    pkuT = pku.T[:, _NPERM]
    pkdT = pkd.T[:, _DPERM]
    pkgR = np.ascontiguousarray(pkgT.reshape(8, 128, 32, 512).transpose(2, 1, 0, 3).reshape(4096, 4096))
    pkuR = np.ascontiguousarray(pkuT.reshape(8, 128, 32, 512).transpose(2, 1, 0, 3).reshape(4096, 4096))
    pkdR = np.ascontiguousarray(pkdT.reshape(16, 4, 128, 2, 1024).transpose(0, 3, 2, 1, 4).reshape(4096, 4096))

    sg = np.asarray(inputs["L0_ffn_gate_scales"], np.float32)[_NPERM, :].T
    su = np.asarray(inputs["L0_ffn_up_scales"], np.float32)[_NPERM, :].T
    sd = np.asarray(inputs["L0_ffn_down_scales"], np.float32)[_DPERM, :].T
    scg = np.ascontiguousarray(np.repeat(
        sg.reshape(8, 8, 32, 512).transpose(1, 2, 0, 3).reshape(8, 32, 4096).transpose(1, 0, 2),
        16, axis=1).reshape(4096, 4096))
    scu = np.ascontiguousarray(np.repeat(
        su.reshape(8, 8, 32, 512).transpose(1, 2, 0, 3).reshape(8, 32, 4096).transpose(1, 0, 2),
        16, axis=1).reshape(4096, 4096))
    scd = np.ascontiguousarray(np.repeat(
        sd.reshape(64, 8, D).transpose(1, 0, 2).reshape(8, 64, 2, 1024)
        .transpose(2, 1, 0, 3).reshape(2, 16, 4, 8, 1024).transpose(0, 1, 3, 2, 4),
        16, axis=2).reshape(4096, 4096))

    ident = np.eye(128, dtype=ml_dtypes.bfloat16)
    qv, kv = np.meshgrid(np.arange(128), np.arange(128), indexing="ij")
    trimask = np.where(kv <= qv, 0.0, NEG).astype(np.float32)
    dist = (np.arange(S) - (S - 1)).astype(np.float32)
    tmt = np.ascontiguousarray(np.tile(trimask, (1, S // 128)))
    alibi = np.ascontiguousarray(np.broadcast_to(dist, (128, S)))
    pgrid = np.arange(128).reshape(128, 1)
    qigrid = np.arange(16).reshape(1, 16)
    negafq = np.zeros((128, 256), np.float32)
    for h in range(NH):
        negafq[:, 16 * h:16 * (h + 1)] = -SLOPES[h] * (128 * qigrid + pgrid - (S - 1))

    shared = dict(wq=wq, wk=wk, wv=wv, wo=wo, bq=bq, bk=bk, bvb=bvb, bob=bob,
                  pkg=pkgR, pku=pkuR, pkd=pkdR, scg=scg, scu=scu, scd=scd,
                  ident=ident, tmt=tmt,
                  negafq=negafq, alibi=alibi)
    xs = [np.ascontiguousarray(x[b][:, _DPERM]) for b in range(NCORE)]
    return shared, xs


def kernel(**inputs):
    shared, xs = host_prep(inputs)
    nc = build()
    in_maps = [dict(shared, x=xs[b]) for b in range(NCORE)]
    res = run_bass_kernel_spmd(nc, in_maps, core_ids=list(range(NCORE)))
    out = np.stack([np.asarray(r["out"], np.float32) for r in res.results])
    return np.ascontiguousarray(out[:, :, _INV_DPERM])
